# version 12
# baseline (speedup 1.0000x reference)
import numpy as np
import ml_dtypes
from contextlib import ExitStack
import concourse.bass as bass
import concourse.mybir as mybir
from concourse.bass_utils import run_bass_kernel_spmd

F32 = mybir.dt.float32
BF16 = mybir.dt.bfloat16
AF = mybir.ActivationFunctionType
ALU = mybir.AluOpType
AX = mybir.AxisListType

T, C = 2048, 1024
NT, NCT = 16, 8
NE = 8
SHIFT = 4.0
NEG = -1.0e9
EPS = 1e-5

ENGS = ("tensor", "scalar", "vector", "sync", "gpsimd")
SEM_OF = {"tensor": "Tm", "scalar": "Sc", "vector": "Ve", "sync": "Ls", "gpsimd": "Lg"}

P_QT, P_KT, P_VAUG, P_U = 0, 2048, 4096, 6176
P_WN, P_WT, P_MASK = 14368, 20512, 26656
P_Q1 = 28704
POOL_N = 30752
ATTN_POOL_KEYS = (["qT", "kT", "vaug"] + [f"U:{i}" for i in range(16)]
                  + ["wn:0", "wn:1", "wt:0", "wt:1"])
MOE_POOL_KEYS = ["wi:0", "wi:1", "wo_e", "rw"]


def btiles(j):
    out = []
    for i in range(16):
        d = 256 + j * 512 - i * 128
        if d >= 128:
            out.append((i, None))
        elif d > -512:
            out.append((i, (-d) // 128))
    return out


class Sched:
    def __init__(self):
        self.prog = {e: [] for e in ENGS}
        self.cnt = {s: 0 for s in ("Tm", "Sc", "Ve", "Ou")}
        self.hwm = {e: {} for e in ENGS}
        self.lw = {}
        self.lu = {}
        self.prev_w = {"vector": (frozenset(), 0), "scalar": (frozenset(), 0)}

    def step(self, eng, emit, reads=(), writes=(), dma=False, out_sem=None):
        sem = out_sem or SEM_OF[eng]
        own = SEM_OF[eng]
        amt = 16 if dma else 1
        if sem not in self.cnt:
            self.cnt[sem] = 0
        want = {}

        emit_own = eng in ("vector", "scalar")

        def add(t):
            if t is None:
                return
            s, v = t
            if s == own and not emit_own:
                return
            want[s] = max(want.get(s, 0), v)

        for k in reads:
            add(self.lw.get(k))
        for k in writes:
            add(self.lw.get(k))
            for s, v in self.lu.get(k, {}).items():
                if s == own and not emit_own:
                    continue
                want[s] = max(want.get(s, 0), v)
        waits = []
        for s, v in want.items():
            if self.hwm[eng].get(s, 0) < v:
                waits.append((s, v))
                self.hwm[eng][s] = v
        need_drain = False
        self_wait = 0
        if eng in ("vector", "scalar"):
            pw, pv = self.prev_w[eng]
            if (set(reads) | set(writes)) & pw:
                self_wait = pv
                need_drain = (eng == "vector")
        self.cnt[sem] += amt
        val = self.cnt[sem]
        if eng in ("vector", "scalar"):
            self.prev_w[eng] = (frozenset(writes), self.cnt[SEM_OF[eng]])
        self.prog[eng].append((tuple(waits), emit, sem, val, need_drain, self_wait))
        for k in writes:
            self.lw[k] = (sem, val)
            self.lu.pop(k, None)
        for k in reads:
            self.lu.setdefault(k, {})[sem] = val
        return val


class MK:
    def __init__(self, n_sub=4, taps=()):
        self.n_sub = n_sub
        self.taps_req = list(taps)
        nc = bass.Bass("TRN2", target_bir_lowering=False)
        self.nc = nc
        self.sc = Sched()
        d = nc.dram_tensor
        self.inp = {}
        for name, shape in [
            ("x", [T, C]),
            ("aqw", [C, C]), ("aqb", [C]), ("akw", [C, C]), ("akb", [C]),
            ("avw", [C, C]), ("avb", [C]), ("aow", [C, C]), ("aob", [C]),
            ("ipw", [3 * C, C]), ("ipb", [3 * C]), ("opw", [C, C]), ("opb", [C]),
            ("rw", [C, NE]), ("win", [NE, C, C]), ("bin", [NE, C]),
            ("wout", [NE, C, C]), ("bout", [NE, C]),
        ]:
            self.inp[name] = d(name, shape, F32, kind="ExternalInput").ap()
        self.inp["ident"] = d("ident", [128, 128], BF16, kind="ExternalInput").ap()
        self.inp["bmask"] = d("bmask", [128, 2048], BF16, kind="ExternalInput").ap()
        self.out = d("out", [T, C], F32, kind="ExternalOutput").ap()
        self.cur_set = None

    def T_(self, emit, **kw):
        return self.sc.step("tensor", emit, **kw)

    def S_(self, emit, **kw):
        return self.sc.step("scalar", emit, **kw)

    def V_(self, emit, **kw):
        return self.sc.step("vector", emit, **kw)

    def DMA(self, queue, emit, dsem="d0", **kw):
        return self.sc.step(queue, emit, dma=True, out_sem=dsem, **kw)

    def act_set(self, func):
        s = {AF.Exp: "exp", AF.Ln: "exp", AF.Gelu_apprx_tanh: "gelu"}.get(func)
        if s is not None and s != self.cur_set:
            self.cur_set = s
            from concourse.hw_specs import get_activation_tables
            names = list(get_activation_tables(self.nc.m.arch).keys())
            idx = names.index({"exp": "natural_log_exp_and_others",
                               "gelu": "gelu_apprx_tanh_and_others"}[s])

            def emit(e, idx=idx):
                e.add_instruction(mybir.InstLoadActFuncSet(
                    name=e.bass.get_next_instruction_name(),
                    act_func_set_id=idx, ins=[], outs=[]))
                return e.drain()
            self.S_(emit)

    def alloc(self, ctx):
        nc = self.nc
        E = ctx.enter_context
        B = {}
        B["x"] = E(nc.sbuf_tensor("xres", [128, NT * 1024], F32))
        B["xlt"] = E(nc.sbuf_tensor("xlt", [128, NCT * 2048], BF16))
        B["big"] = E(nc.sbuf_tensor("big", [128, NCT * 2048], BF16))
        B["pool"] = E(nc.sbuf_tensor("pool", [128, POOL_N], BF16))
        B["lnrow"] = E(nc.sbuf_tensor("lnrow", [128, 2 * 1024], BF16))
        B["ident"] = E(nc.sbuf_tensor("identsb", [128, 128], BF16))
        B["ones"] = E(nc.sbuf_tensor("ones", [1, 512], BF16))
        B["st"] = E(nc.sbuf_tensor("st", [128, 64], F32))
        B["kmT"] = E(nc.sbuf_tensor("kmT", [128, 16], F32))
        B["bcol"] = E(nc.sbuf_tensor("bcol", [128, 24], BF16))
        B["bincol"] = E(nc.sbuf_tensor("bincol", [128, 64], BF16))
        B["bstage"] = E(nc.sbuf_tensor("bstage", [8, 1024], BF16))
        B["obias"] = E(nc.sbuf_tensor("obias", [1, 1024], BF16))
        B["probs"] = E(nc.sbuf_tensor("probs", [128, 128], F32))
        B["gates"] = E(nc.sbuf_tensor("gates", [128, 128], F32))
        B["r16"] = E(nc.sbuf_tensor("r16", [128, 32], F32))
        B["mx8"] = E(nc.sbuf_tensor("mx8", [128, 8], F32))
        B["rinv"] = E(nc.sbuf_tensor("rinv", [1, 512], BF16))
        B["rln"] = E(nc.sbuf_tensor("rln", [1, 512], BF16))
        B["consts"] = E(nc.sbuf_tensor("consts", [128, 2], F32))
        B["bcs"] = E(nc.sbuf_tensor("bcs", [64, 512], BF16))
        B["psA"] = E(nc.psum_tensor("psA", [128, 1024], F32))
        B["psB"] = E(nc.psum_tensor("psB", [128, 1024], F32))
        B["psC"] = E(nc.psum_tensor("psC", [128, 512], F32))
        B["psD"] = E(nc.psum_tensor("psD", [128, 512], F32))
        B["psE"] = E(nc.psum_tensor("psE", [128, 512], F32))
        B["psT"] = E(nc.psum_tensor("psT", [128, 1024], BF16))
        self.B = B
        self.xv = B["x"][:, :].rearrange("p (i c) -> p i c", i=NT)
        self.xltv = B["xlt"][:, :].rearrange("p (k t) -> p k t", k=NCT)
        self.bigv = B["big"][:, :].rearrange("p (k t) -> p k t", k=NCT)

    def pool(self, lo, hi):
        return self.B["pool"][:, lo:hi]

    def prologue(self):
        B = self.B

        def e_x(e):
            return e.dma_start(out=self.xv,
                               in_=self.inp["x"].rearrange("(i p) c -> p i c", p=128))
        self.DMA("sync", e_x, dsem="dx", writes=[f"x:{i}" for i in range(NT)])

        self.DMA("sync", lambda e: e.dma_start(out=B["ident"][:, :], in_=self.inp["ident"]),
                 dsem="dident", writes=["ident"])
        self.V_(lambda e: e.memset(B["ones"][0:1, :], 1.0), writes=["ones"])
        self.V_(lambda e: e.memset(B["consts"][:, 0:1], EPS), writes=["consts"])
        self.V_(lambda e: e.memset(B["consts"][:, 1:2], -SHIFT), writes=["consts"])

    def ln_phase(self, kmask):
        B = self.B
        st = B["st"]
        for i in range(NT):
            sl = (i % 4) * 16
            xt = self.xv[:, i, :]
            def e_bn(e, xt=xt, sl=sl):
                e.bn_stats(st[:, sl + 4:sl + 10], xt[:, 0:512])
                return e.bn_stats(st[:, sl + 10:sl + 16], xt[:, 512:1024])
            self.V_(e_bn, reads=[f"x:{i}"], writes=[f"st:{i % 4}"])

            def e_ag(e, sl=sl):
                return e.bn_aggr(st[:, sl + 0:sl + 2], st[:, sl + 4:sl + 16])
            self.V_(e_ag, reads=[f"st:{i % 4}"], writes=[f"st:{i % 4}"])
            self.act_set(AF.Ln)

            def e_ln1(e, sl=sl):
                return e.activation(st[:, sl + 2:sl + 3], st[:, sl + 1:sl + 2], AF.Ln,
                                    bias=self.B["consts"][:, 0:1])
            self.S_(e_ln1, reads=[f"st:{i % 4}", "consts"], writes=[f"st:{i % 4}"])

            def e_ln2(e, sl=sl):
                return e.activation(st[:, sl + 3:sl + 4], st[:, sl + 2:sl + 3],
                                    AF.Exp, scale=-0.5)
            self.S_(e_ln2, reads=[f"st:{i % 4}"], writes=[f"st:{i % 4}"])
            def e_nm(e, sl=sl):
                return e.tensor_scalar(st[:, sl + 0:sl + 1], st[:, sl + 0:sl + 1],
                                       st[:, sl + 3:sl + 4], -1.0,
                                       op0=ALU.mult, op1=ALU.mult)
            self.V_(e_nm, reads=[f"st:{i % 4}"], writes=[f"st:{i % 4}"])
            lr = B["lnrow"][:, (i % 2) * 1024:(i % 2) * 1024 + 1024]

            def e_ap(e, xt=xt, lr=lr, sl=sl):
                return e.activation(lr, xt, AF.Identity, scale=st[:, sl + 3:sl + 4],
                                    bias=st[:, sl + 0:sl + 1])
            self.S_(e_ap, reads=[f"x:{i}", f"st:{i % 4}"], writes=[f"lnrow:{i % 2}"])
            if kmask:
                def e_km(e, lr=lr, i=i):
                    return e.tensor_scalar(B["kmT"][:, i:i + 1], lr[:, 0:1], 0.0,
                                           None, op0=ALU.not_equal)
                self.V_(e_km, reads=[f"lnrow:{i % 2}"], writes=["kmT"])
            def e_tr(e, lr=lr):
                ins = None
                for k in range(8):
                    ins = e.transpose(B["psT"][:, k * 128:(k + 1) * 128],
                                      lr[:, k * 128:(k + 1) * 128], B["ident"][:, :])
                return ins
            self.T_(e_tr, reads=[f"lnrow:{i % 2}", "ident"], writes=["psT"])

            def e_ev(e, i=i):
                src = B["psT"][:, :].rearrange("p (k t) -> p k t", k=8)
                dst = self.xltv[:, :, i * 128:(i + 1) * 128]
                return e.tensor_copy(dst, src)
            self.V_(e_ev, reads=["psT"], writes=[f"xlt:{i // 4}", "psT"])

    def attn_phase(self, cfg):
        B = self.B
        nc = self.nc
        band = cfg["band"]
        ob_ap = self.inp[cfg["ob"]].rearrange("(o c) -> o c", o=1)
        self.DMA("gpsimd", lambda e, a=ob_ap: e.dma_start(out=B["obias"][0:1, :], in_=a),
                 dsem="dob", writes=["obias"])
        vv = self.pool(P_VAUG, P_VAUG + 16 * 130).rearrange("p (t s) -> p t s", t=16)
        wx = ["wi:0"] if cfg.get("fence_moe") else []
        self.V_(lambda e, v=vv: e.memset(v[:, :, 64:65], 1.0), writes=["vaug"] + wx)
        self.V_(lambda e, v=vv: e.memset(v[:, :, 129:130], 1.0), writes=["vaug"])
        q0 = self.pool(P_QT, P_QT + 2048)
        q1 = self.pool(P_Q1, P_Q1 + 2048)
        self.V_(lambda e, a=q0: e.memset(a[64:128, :], 0.0), writes=["qT"])
        self.V_(lambda e, a=q1: e.memset(a[0:64, :], 0.0), writes=["qT"])
        if cfg["band"]:
            mask_ap = self.pool(P_MASK, P_MASK + 2048)
            self.DMA("sync", lambda e, m=mask_ap: e.dma_start(out=m, in_=self.inp["bmask"]),
                     dsem="dmask", writes=["mask"])
        self.bias_cols_all(cfg)
        for p in range(9):
            if p < 8:
                self.wn_load(cfg, p)
            if p == 1:
                self.wt_transpose(0)
            if p > 0:
                self.qkv_pair(cfg, p - 1)
                self.core_pair(cfg, p - 1, band,
                               host_wt=(p if p < 8 else None))
        self.o_proj(cfg)

    def wn_load(self, cfg, p):
        b = p % 2
        fence = list(MOE_POOL_KEYS) if (p == 0 and cfg.get("fence_moe")) else []
        dst = self.pool(P_WN + b * 3072, P_WN + b * 3072 + 3072)
        dstv = dst.rearrange("p (w c) -> p w c", w=3)
        for w, src in enumerate(cfg["qkv_w"]):
            ap = src[p * 128:(p + 1) * 128, :]

            def e(e_, d=dstv[:, w, :], a=ap):
                return e_.dma_start(out=d, in_=a)
            self.DMA("gpsimd", e, dsem=f"dwn{b}", writes=[f"wn:{b}"] + fence)
            fence = []

    def bias_cols_all(self, cfg):
        B = self.B
        for w, srcb in enumerate(cfg["qkv_b"]):
            ap = srcb.rearrange("(k p) -> k p", k=8)

            def e(e_, d=B["bstage"][0:8, w * 128:(w + 1) * 128], a=ap):
                return e_.dma_start(out=d, in_=a)
            self.DMA("gpsimd", e, dsem="dbst0", writes=["bst"])

        def e_tr(e):
            ins = None
            for w in range(3):
                ins = e.transpose(B["psT"][:, w * 8:(w + 1) * 8],
                                  B["bstage"][0:8, w * 128:(w + 1) * 128],
                                  B["ident"][0:8, 0:8])
            return ins
        self.T_(e_tr, reads=["bst", "ident"], writes=["psT"])

        def e_ev(e):
            return e.tensor_copy(B["bcol"][:, 0:24], B["psT"][:, 0:24])
        self.V_(e_ev, reads=["psT"], writes=["bcol", "psT"])

    def wt_transpose(self, p, only_w=None):
        B = self.B
        b = p % 2
        src = self.pool(P_WN + b * 3072, P_WN + b * 3072 + 3072)
        srcv = src.rearrange("p (w c) -> p w c", w=3)
        dst = self.pool(P_WT + b * 3072, P_WT + b * 3072 + 3072)
        dstv = dst.rearrange("p (w k d) -> p w k d", w=3, k=8)
        for w in (range(3) if only_w is None else [only_w]):
            def e_tr(e, w=w, b=b):
                ins = None
                for k in range(8):
                    ins = e.transpose(B["psT"][:, k * 128:(k + 1) * 128],
                                      srcv[:, w, k * 128:(k + 1) * 128],
                                      B["ident"][:, :])
                return ins
            self.T_(e_tr, reads=[f"wn:{b}", "ident"], writes=["psT"])

            def e_ev(e, w=w):
                return e.tensor_copy(dstv[:, w, :, :],
                                     B["psT"][:, :].rearrange("p (k d) -> p k d", k=8))
            self.V_(e_ev, reads=["psT"], writes=[f"wt:{b}", "psT"])

    def qkv_pair(self, cfg, p):
        B = self.B
        b = p % 2
        wtv = self.pool(P_WT + b * 3072, P_WT + b * 3072 + 3072).rearrange(
            "p (w k d) -> p w k d", w=3, k=8)
        q0 = self.pool(P_QT, P_QT + 2048)
        q1 = self.pool(P_Q1, P_Q1 + 2048)
        kT = self.pool(P_KT, P_KT + 2048)
        ureg = self.pool(P_U, P_U + 8192)
        dests = [("qT", None), ("kT", kT), ("vT", ureg[:, 0:2048])]
        for w, (dkey, dap) in enumerate(dests):
            for half in range(2):
                ps = B["psA"] if half == 0 else B["psB"]
                pskeys = ["psA0", "psA1"] if half == 0 else ["psB0", "psB1"]
                for cc in range(2):
                    chunk = half * 2 + cc

                    def e_mm(e, w=w, cc=cc, chunk=chunk, ps=ps, b=b):
                        ins = None
                        for k in range(8):
                            ins = e.matmul(ps[:, cc * 512:(cc + 1) * 512],
                                           wtv[:, w, k, :],
                                           self.xltv[:, k, chunk * 512:(chunk + 1) * 512],
                                           start=(k == 0), stop=(k == 7))
                        return ins
                    self.T_(e_mm, reads=[f"wt:{b}", f"xlt:{chunk}"],
                            writes=[pskeys[cc]])
                wkeys = [dkey] if dkey != "vT" else ["U:0", "U:1", "U:2", "U:3"]

                bc = w * 8 + p

                def e_ev(e, dap=dap, half=half, ps=ps, bc=bc):
                    if dap is None:
                        e.activation(q0[0:64, half * 1024:(half + 1) * 1024],
                                     ps[0:64, :], AF.Identity,
                                     bias=B["bcol"][0:64, bc:bc + 1])
                        return e.activation(q1[64:128, half * 1024:(half + 1) * 1024],
                                            ps[64:128, :], AF.Identity,
                                            bias=B["bcol"][64:128, bc:bc + 1])
                    return e.activation(dap[:, half * 1024:(half + 1) * 1024],
                                        ps[:, :], AF.Identity,
                                        bias=B["bcol"][:, bc:bc + 1])
                self.S_(e_ev, reads=pskeys + ["bcol"], writes=wkeys + pskeys)
        vv = self.pool(P_VAUG, P_VAUG + 16 * 130).rearrange("p (t s) -> p t s", t=16)
        for half in range(2):
            def e_tr(e, half=half):
                ins = None
                for k in range(8):
                    ins = e.transpose(B["psT"][:, k * 128:(k + 1) * 128],
                                      ureg[:, half * 1024 + k * 128:half * 1024 + (k + 1) * 128],
                                      B["ident"][:, :])
                return ins
            self.T_(e_tr, reads=["U:0", "U:1", "U:2", "U:3", "ident"], writes=["psT"])

            def e_ev(e, half=half):
                src = B["psT"][:, :].rearrange("p (t g x) -> p t g x", t=8, g=2)
                dst = vv[:, half * 8:(half + 1) * 8, :].rearrange(
                    "p t (g y) -> p t g y", g=2)[:, :, :, 0:64]
                return e.tensor_copy(dst, src)
            self.V_(e_ev, reads=["psT"], writes=["vaug", "psT"])

    def core_pair(self, cfg, p, band, host_wt=None):
        B = self.B
        qbufs = (self.pool(P_QT, P_QT + 2048), self.pool(P_Q1, P_Q1 + 2048))
        kT = self.pool(P_KT, P_KT + 2048)
        ureg = self.pool(P_U, P_U + 8192)
        uv = ureg.rearrange("p (h s) -> p h s", h=2)
        vv = self.pool(P_VAUG, P_VAUG + 16 * 130).rearrange("p (t s) -> p t s", t=16)
        maskv = self.pool(P_MASK, P_MASK + 2048)
        self.act_set(AF.Exp)
        R = 8
        for j in range(4):
            if host_wt is not None and j >= 1:
                self.wt_transpose(host_wt, only_w=j - 1)
            tiles = btiles(j) if band else [(i, None) for i in range(16)]
            n = len(tiles)

            def ukeys(g):
                return [f"U:{g % R}", f"U:{8 + g % R}"]

            def emit_scores(g):
                i, mi = tiles[g]
                ps = B["psA"] if g % 2 == 0 else B["psB"]
                pk = "psA" if g % 2 == 0 else "psB"
                sl = (g % R) * 512

                def e_mm(e, i=i, j=j, ps=ps):
                    e.matmul(ps[:, 0:512], kT[:, i * 128:(i + 1) * 128],
                             qbufs[0][:, j * 512:(j + 1) * 512],
                             start=True, stop=True)
                    return e.matmul(ps[:, 512:1024], kT[:, i * 128:(i + 1) * 128],
                                    qbufs[1][:, j * 512:(j + 1) * 512],
                                    start=True, stop=True)
                self.T_(e_mm, reads=["qT", "kT"], writes=[pk + "0", pk + "1"])
                uout = uv[:, :, sl:sl + 512]

                def e_ex(e, ps=ps, uout=uout):
                    return e.activation(uout, ps[:, :], AF.Exp, scale=0.125,
                                        bias=self.B["consts"][:, 1:2])
                self.S_(e_ex, reads=[pk + "0", pk + "1", "consts"],
                        writes=ukeys(g) + [pk + "0", pk + "1"])
                if not band:
                    def e_mk(e, uout=uout, i=i):
                        return e.tensor_scalar(uout, uout, B["kmT"][:, i:i + 1],
                                               None, op0=ALU.mult)
                    self.V_(e_mk, reads=ukeys(g) + ["kmT"], writes=ukeys(g))
                elif mi is not None:
                    def e_mkb(e, uout=uout, mi=mi):
                        e.tensor_tensor(uout[:, 0, :], uout[:, 0, :],
                                        maskv[:, mi * 512:(mi + 1) * 512],
                                        op=ALU.mult)
                        return e.tensor_tensor(uout[:, 1, :], uout[:, 1, :],
                                               maskv[:, mi * 512:(mi + 1) * 512],
                                               op=ALU.mult)
                    self.V_(e_mkb, reads=ukeys(g) + ["mask"], writes=ukeys(g))

            def emit_pv(g):
                i, mi = tiles[g]
                sl = (g % R) * 512

                def e_pv(e, g=g, i=i, sl=sl, n=n):
                    e.matmul(B["psC"][0:65, :], vv[:, i, 0:65],
                             uv[:, 0, sl:sl + 512],
                             start=(g == 0), stop=(g == n - 1))
                    return e.matmul(B["psD"][0:65, :], vv[:, i, 65:130],
                                    uv[:, 1, sl:sl + 512],
                                    start=(g == 0), stop=(g == n - 1))
                self.T_(e_pv, reads=["vaug"] + ukeys(g), writes=["psC", "psD"])

            for g in range(n):
                emit_scores(g)
                if g >= 2:
                    emit_pv(g - 2)
            emit_pv(n - 2)
            emit_pv(n - 1)
            for h, pvk in ((0, "psC"), (1, "psD")):
                pv = B[pvk]

                def e_ri1(e, pv=pv):
                    return e.activation(B["rln"][0:1, :], pv[64:65, :], AF.Ln)
                self.S_(e_ri1, reads=[pvk], writes=["rln"])

                def e_ri2(e):
                    with self.nc.allow_low_precision(
                            reason="bf16 softmax denominator for bf16 PV product"):
                        return e.activation(B["rinv"][0:1, :], B["rln"][0:1, :],
                                            AF.Exp, scale=-1.0)
                self.S_(e_ri2, reads=["rln"], writes=["rinv"])

                def e_bc(e):
                    return e.matmul(B["psE"][0:64, :], B["ones"][0:1, 0:64],
                                    B["rinv"][0:1, :], start=True, stop=True)
                self.T_(e_bc, reads=["rinv", "ones"], writes=["psE"])

                def e_cp(e):
                    return e.tensor_copy(B["bcs"][0:64, :], B["psE"][0:64, :])
                self.V_(e_cp, reads=["psE"], writes=["bcs", "psE"])

                def e_mu(e, pv=pv, p=p, h=h, j=j):
                    dst = self.bigv[h * 64:(h + 1) * 64, p, j * 512:(j + 1) * 512]
                    return e.tensor_tensor(dst, pv[0:64, :], B["bcs"][0:64, :],
                                           op=ALU.mult)
                self.V_(e_mu, reads=[pvk, "bcs"],
                        writes=[f"big:{4 * j + q}" for q in range(4)] + [pvk])

    def o_proj(self, cfg):
        B = self.B
        wof = cfg["ow"]
        woT = self.pool(P_U, P_U + 8192).rearrange("p (k d) -> p k d", k=8)
        ukeys = [f"U:{i}" for i in range(16)]
        for s in range(8):
            b = s % 2
            dst = self.pool(P_WN + b * 3072, P_WN + b * 3072 + 1024)

            def e_dma(e, dst=dst, s=s):
                return e.dma_start(out=dst, in_=wof[s * 128:(s + 1) * 128, :])
            self.DMA("gpsimd", e_dma, dsem=f"dwn{b}", writes=[f"wn:{b}"])

            def e_tr(e, dst=dst):
                ins = None
                for k in range(8):
                    ins = e.transpose(B["psT"][:, k * 128:(k + 1) * 128],
                                      dst[:, k * 128:(k + 1) * 128], B["ident"][:, :])
                return ins
            self.T_(e_tr, reads=[f"wn:{b}", "ident"], writes=["psT"])

            def e_ev(e, s=s):
                src = B["psT"][:, :].rearrange("p (k d) -> p k d", k=8)
                return e.tensor_copy(woT[:, :, s * 128:(s + 1) * 128], src)
            self.V_(e_ev, reads=["psT"],
                    writes=["psT", "woT"] + (ukeys if s == 0 else []))
        opl = B["lnrow"][:, 0:1024]
        for db in range(2):
            def e_bc(e, db=db):
                return e.matmul(B["psE"][:, :], B["ones"][0:1, 0:128],
                                B["obias"][0:1, db * 512:(db + 1) * 512],
                                start=True, stop=True)
            self.T_(e_bc, reads=["obias", "ones"], writes=["psE"])

            def e_cp(e, db=db):
                return e.tensor_copy(opl[:, db * 512:(db + 1) * 512], B["psE"][:, :])
            self.V_(e_cp, reads=["psE"], writes=["lnrow:0", "psE"])
        for tt in range(NT):
            ps = B["psA"] if tt % 2 == 0 else B["psB"]
            pk = "psA" if tt % 2 == 0 else "psB"
            for db in range(2):
                def e_mm(e, tt=tt, db=db, ps=ps):
                    ins = None
                    for k in range(8):
                        ins = e.matmul(ps[:, db * 512:(db + 1) * 512],
                                       self.bigv[:, k, tt * 128:(tt + 1) * 128],
                                       woT[:, k, db * 512:(db + 1) * 512],
                                       start=(k == 0), stop=(k == 7))
                    return ins
                self.T_(e_mm, reads=["woT", f"big:{tt}"]
                        + [f"U:{i}" for i in range(16)],
                        writes=[pk + str(db)])

            def e_add1(e, tt=tt, ps=ps):
                return e.tensor_tensor(self.xv[:, tt, :], self.xv[:, tt, :],
                                       ps[:, :], op=ALU.add)
            self.V_(e_add1, reads=[pk + "0", pk + "1", f"x:{tt}"],
                    writes=[f"x:{tt}", pk + "0", pk + "1"])

            def e_add2(e, tt=tt):
                return e.tensor_tensor(self.xv[:, tt, :], self.xv[:, tt, :],
                                       opl[:, :], op=ALU.add)
            self.V_(e_add2, reads=[f"x:{tt}", "lnrow:0"], writes=[f"x:{tt}"])

    def moe_phase(self, first):
        B = self.B
        wiv = [self.pool(P_QT + b * 8192, P_QT + b * 8192 + 8192).rearrange(
            "p (k h) -> p k h", k=8) for b in range(2)]
        wov = self.pool(16384, 16384 + 8192).rearrange("p (k d) -> p k d", k=8)
        rwv = self.pool(24576, 24576 + 64).rearrange("p (k e) -> p k e", k=8)
        fence = list(ATTN_POOL_KEYS)

        def dma_wi(e_idx):
            b = e_idx % 2

            def e(e_, b=b, e_idx=e_idx):
                return e_.dma_start(out=wiv[b][:, :, :],
                                    in_=self.inp["win"][e_idx:e_idx + 1, :, :].rearrange(
                                        "o (k p) h -> p (o k) h", p=128))
            nonlocal fence
            self.DMA("gpsimd", e, dsem=f"dwi{b}", writes=[f"wi:{b}"] + fence)
            fence = []

        def bias_all():
            def e_bi(e_):
                return e_.dma_start(
                    out=B["bstage"][0:8, :].rearrange("k (e p) -> k e p", e=8),
                    in_=self.inp["bin"].rearrange("e (k p) -> k e p", k=8))
            self.DMA("gpsimd", e_bi, dsem="dbst1", writes=["bst"])

            def e_tr(e_):
                ins = None
                for ei in range(8):
                    ins = e_.transpose(B["psT"][:, ei * 8:(ei + 1) * 8],
                                       B["bstage"][0:8, ei * 128:(ei + 1) * 128],
                                       B["ident"][0:8, 0:8])
                return ins
            self.T_(e_tr, reads=["bst", "ident"], writes=["psT"])

            def e_ev(e_):
                return e_.tensor_copy(B["bincol"][:, 0:64], B["psT"][:, 0:64])
            self.V_(e_ev, reads=["psT"], writes=["binc", "psT"])


        def e_rw(e_):
            return e_.dma_start(out=rwv[:, :, :],
                                in_=self.inp["rw"].rearrange("(k p) e -> p k e", p=128))
        self.DMA("gpsimd", e_rw, dsem="drw", writes=["rw"] + fence)
        fence = []
        dma_wi(0)
        bias_all()
        for tt in range(NT):
            def e_mm(e, tt=tt):
                ins = None
                for k in range(8):
                    ins = e.matmul(B["psC"][:, tt * 8:(tt + 1) * 8],
                                   self.xltv[:, k, tt * 128:(tt + 1) * 128],
                                   rwv[:, k, :], start=(k == 0), stop=(k == 7))
                return ins
            self.T_(e_mm, reads=["rw", f"xlt:{tt // 4}"], writes=["psC"])
        self.act_set(AF.Exp)
        self.S_(lambda e: e.activation(B["probs"][:, :], B["psC"][:, 0:128], AF.Exp),
                reads=["psC"], writes=["probs", "psC"])
        pv = B["probs"][:, :].rearrange("p (t e) -> p t e", t=16)
        self.V_(lambda e: e.reduce_sum(B["r16"][:, 0:16], pv, AX.X),
                reads=["probs"], writes=["r16"])
        self.V_(lambda e: e.reciprocal(B["r16"][:, 16:32], B["r16"][:, 0:16]),
                reads=["r16"], writes=["r16"])
        gT = B["pool"][0:8, P_MASK:P_MASK + 2048]
        boall = B["pool"][0:8, 24640:25664]
        gbf = B["lnrow"][:, 1024:1152]

        def e_ba(e_):
            return e_.dma_start(out=boall, in_=self.inp["bout"])
        self.DMA("gpsimd", e_ba, dsem="dbo", writes=["wt:1"])
        for tt in range(NT):
            def e_g1(e, tt=tt):
                return e.max(out=B["mx8"][:, :], in_=B["probs"][:, tt * 8:(tt + 1) * 8])
            self.V_(e_g1, reads=["probs"], writes=["mx8"])

            def e_g2(e, tt=tt):
                return e.scalar_tensor_tensor(
                    B["gates"][:, tt * 8:(tt + 1) * 8],
                    B["probs"][:, tt * 8:(tt + 1) * 8], B["mx8"][:, 1:2],
                    B["probs"][:, tt * 8:(tt + 1) * 8], op0=ALU.is_ge, op1=ALU.mult)
            self.V_(e_g2, reads=["probs", "mx8"], writes=["gates"])

            def e_g3(e, tt=tt):
                return e.tensor_scalar(B["gates"][:, tt * 8:(tt + 1) * 8],
                                       B["gates"][:, tt * 8:(tt + 1) * 8],
                                       B["r16"][:, 16 + tt:17 + tt], None, op0=ALU.mult)
            self.V_(e_g3, reads=["gates", "r16"], writes=["gates"])
        self.V_(lambda e: e.tensor_copy(gbf, B["gates"][:, :]),
                reads=["gates"], writes=["lnrow:1"])
        for fill in range(2):
            def e_tr(e, fill=fill):
                ins = None
                for i in range(8):
                    tt = fill * 8 + i
                    ins = e.transpose(B["psT"][0:8, i * 128:(i + 1) * 128],
                                      gbf[:, tt * 8:(tt + 1) * 8],
                                      B["ident"][:, :])
                return ins
            self.T_(e_tr, reads=["lnrow:1", "ident"], writes=["psT"])

            def e_ev(e, fill=fill):
                return e.tensor_copy(gT[0:8, fill * 1024:(fill + 1) * 1024],
                                     B["psT"][0:8, 0:1024])
            self.V_(e_ev, reads=["psT"], writes=["mask", "psT"])
        for ei in range(NE):
            b = ei % 2
            def e_wo(e_, ei=ei):
                return e_.dma_start(out=wov[:, :, :],
                                    in_=self.inp["wout"][ei:ei + 1, :, :].rearrange(
                                        "o (k p) d -> p (o k) d", p=128))
            self.DMA("gpsimd", e_wo, dsem="dwo", writes=["wo_e"])
            if ei + 1 < NE:
                dma_wi(ei + 1)
            self.act_set(AF.Gelu_apprx_tanh)
            for mt in range(8):
                for half in range(2):
                    ps = B["psA"] if half == 0 else B["psB"]
                    pk = "psA" if half == 0 else "psB"
                    for cc in range(2):
                        chunk = half * 2 + cc

                        def e_mm(e, mt=mt, cc=cc, chunk=chunk, ps=ps, b=b):
                            ins = None
                            for k in range(8):
                                ins = e.matmul(ps[:, cc * 512:(cc + 1) * 512],
                                               wiv[b][:, k, mt * 128:(mt + 1) * 128],
                                               self.xltv[:, k, chunk * 512:(chunk + 1) * 512],
                                               start=(k == 0), stop=(k == 7))
                            return ins
                        self.T_(e_mm, reads=[f"wi:{b}", f"xlt:{chunk}"],
                                writes=[pk + str(cc)])

                    def e_gl(e, mt=mt, half=half, ps=ps, ei=ei):
                        return e.activation(self.bigv[:, mt, half * 1024:(half + 1) * 1024],
                                            ps[:, :], AF.Gelu_apprx_tanh,
                                            bias=B["bincol"][:, ei * 8 + mt:ei * 8 + mt + 1])
                    self.S_(e_gl, reads=[pk + "0", pk + "1", "binc"],
                            writes=[pk + "0", pk + "1"]
                            + [f"big:{half * 8 + q}" for q in range(8)])
            ykeys = ("psC", "psD", "psE")
            for tt in range(NT):
                for db in range(2):
                    yk = ykeys[(tt * 2 + db) % 3]
                    ps = B[yk]

                    def e_mm(e, tt=tt, db=db, ps=ps):
                        ins = None
                        for k in range(8):
                            ins = e.matmul(ps[:, :],
                                           self.bigv[:, k, tt * 128:(tt + 1) * 128],
                                           wov[:, k, db * 512:(db + 1) * 512],
                                           start=(k == 0), stop=(k == 7))
                        return ins
                    self.T_(e_mm, reads=["wo_e", f"big:{tt}"], writes=[yk])

                    def e_acc(e, tt=tt, db=db, ps=ps, ei=ei):
                        dst = self.xv[:, tt, db * 512:(db + 1) * 512]
                        return e.scalar_tensor_tensor(
                            dst, ps[:, :], B["gates"][:, tt * 8 + ei:tt * 8 + ei + 1],
                            dst, op0=ALU.mult, op1=ALU.add)
                    self.V_(e_acc, reads=[yk, "gates", f"x:{tt}"],
                            writes=[f"x:{tt}", yk])
        for tt in range(NT):
            for db in range(2):
                yk = ykeys[(tt * 2 + db) % 3]
                ps = B[yk]

                def e_mm(e, tt=tt, db=db, ps=ps, gT=gT, boall=boall):
                    return e.matmul(ps[:, :], gT[0:8, tt * 128:(tt + 1) * 128],
                                    boall[0:8, db * 512:(db + 1) * 512],
                                    start=True, stop=True)
                self.T_(e_mm, reads=["mask", "wt:1"], writes=[yk])

                def e_add(e, tt=tt, db=db, ps=ps):
                    dst = self.xv[:, tt, db * 512:(db + 1) * 512]
                    return e.tensor_tensor(dst, dst, ps[:, :], op=ALU.add)
                self.V_(e_add, reads=[yk, f"x:{tt}"], writes=[f"x:{tt}", yk])

    def tap_spec(self, name):
        B = self.B
        specs = {
            "st": (B["st"][:, :], [128, 64], F32),
            "kmT": (B["kmT"][:, :], [128, 16], F32),
            "consts": (B["consts"][:, :], [128, 2], F32),
            "xlt0": (B["xlt"][:, 0:2048], [128, 2048], BF16),
            "xlt": (B["xlt"][:, :], [128, NCT * 2048], BF16),
            "lnrow": (B["lnrow"][:, :], [128, 2048], BF16),
            "qT": (self.pool(P_QT, P_QT + 2048), [128, 2048], BF16),
            "kT": (self.pool(P_KT, P_KT + 2048), [128, 2048], BF16),
            "vaug": (self.pool(P_VAUG, P_VAUG + 2080), [128, 2080], BF16),
            "u": (self.pool(P_U, P_U + 8192), [128, 8192], BF16),
            "wt0": (self.pool(P_WT, P_WT + 3072), [128, 3072], BF16),
            "bcol": (B["bcol"][:, :], [128, 8], BF16),
            "big": (B["big"][:, :], [128, NCT * 2048], BF16),
            "probs": (B["probs"][:, :], [128, 128], F32),
            "gates": (B["gates"][:, :], [128, 128], F32),
            "brow": (B["brow"][:, :], [1, 2048], BF16),
            "bincol": (B["bincol"][:, :], [128, 16], BF16),
        }
        return specs[name]

    def epilogue(self):
        def e_out(e):
            return e.dma_start(out=self.out.rearrange("(i p) c -> p i c", p=128),
                               in_=self.xv)
        self.DMA("sync", e_out, dsem="Ou", reads=[f"x:{i}" for i in range(NT)])
        allkeys = list(self.sc.lw.keys())
        for name in self.taps_req:
            ap, shape, dt = self.tap_spec(name)
            o = self.nc.dram_tensor("tap_" + name, shape, dt,
                                    kind="ExternalOutput").ap()

            def e_t(e, o=o, a=ap):
                return e.dma_start(out=o, in_=a)
            self.DMA("sync", e_t, dsem="Ou", reads=allkeys)

    def build(self):
        nc = self.nc
        with ExitStack() as ctx:
            self.alloc(ctx)
            self.prologue()
            cfg_a = dict(band=False, fence_moe=False,
                         qkv_w=[self.inp["aqw"], self.inp["akw"], self.inp["avw"]],
                         qkv_b=[self.inp["aqb"], self.inp["akb"], self.inp["avb"]],
                         ow=self.inp["aow"], ob="aob")
            ip, ipb = self.inp["ipw"], self.inp["ipb"]
            cfg_b = dict(band=True, fence_moe=True,
                         qkv_w=[ip[0:C, :], ip[C:2 * C, :], ip[2 * C:3 * C, :]],
                         qkv_b=[ipb[0:C], ipb[C:2 * C], ipb[2 * C:3 * C]],
                         ow=self.inp["opw"], ob="opb")
            subs = [("attn", cfg_a), ("moe", None), ("attn", cfg_b), ("moe", None)]
            for si, (kind, cfg) in enumerate(subs[: self.n_sub]):
                self.act_set(AF.Ln)
                self.ln_phase(kmask=(kind == "attn" and not (cfg or {}).get("band")))
                if kind == "attn":
                    self.attn_phase(cfg)
                else:
                    self.moe_phase(first=(si == 1))
            self.epilogue()

            S = {}
            for s in self.sc.cnt:
                S[s] = ctx.enter_context(nc.semaphore(s))
            blk = ctx.enter_context(nc.Block())

            def make(eng_name):
                steps = self.sc.prog[eng_name]

                own = SEM_OF[eng_name]

                def body(e):
                    for waits, emit, sem, val, need_drain, self_wait in steps:
                        for s, v in waits:
                            e.wait_ge(S[s], v)
                        if self_wait:
                            e.wait_ge(S[own], self_wait)
                        if need_drain:
                            e.drain()
                        ins = emit(e)
                        if ins is None:
                            ins = e.nop()
                        ins.then_inc(S[sem], 1 if sem in ("Tm", "Sc", "Ve") else 16)
                    if eng_name == "sync":
                        e.wait_ge(S["Ou"], self.sc.cnt["Ou"])
                return body
            blk.tensor(make("tensor"))
            blk.scalar(make("scalar"))
            blk.vector(make("vector"))
            blk.sync(make("sync"))
            blk.gpsimd(make("gpsimd"))
        return nc


def _masks():
    m = np.zeros((128, 2048), dtype=np.float32)
    for mi, d in enumerate((0, -128, -256, -384)):
        for i in range(128):
            m[i, mi * 512:(mi + 1) * 512] = (i <= np.arange(512) + d)
    return m.astype(ml_dtypes.bfloat16)


_CACHE = {}


def _get_nc(n_sub=4, taps=()):
    key = (n_sub, tuple(taps))
    if key not in _CACHE:
        _CACHE[key] = MK(n_sub, taps).build()
    return _CACHE[key]


def kernel(**inputs):
    nc = _get_nc(4)
    base = {
        "aqw": inputs["L0_attn_a_wq_w"], "aqb": inputs["L0_attn_a_wq_b"],
        "akw": inputs["L0_attn_a_wk_w"], "akb": inputs["L0_attn_a_wk_b"],
        "avw": inputs["L0_attn_a_wv_w"], "avb": inputs["L0_attn_a_wv_b"],
        "aow": inputs["L0_attn_a_wo_w"], "aob": inputs["L0_attn_a_wo_b"],
        "ipw": inputs["L0_attn_b_in_proj_w"], "ipb": inputs["L0_attn_b_in_proj_b"],
        "opw": inputs["L0_attn_b_out_proj_w"], "opb": inputs["L0_attn_b_out_proj_b"],
        "rw": inputs["L0_ffn_router_w"], "win": inputs["L0_ffn_w_in"],
        "bin": inputs["L0_ffn_b_in"], "wout": inputs["L0_ffn_w_out"],
        "bout": inputs["L0_ffn_b_out"],
        "ident": np.eye(128, dtype=np.float32).astype(ml_dtypes.bfloat16),
        "bmask": _masks(),
    }
    base = {k: np.ascontiguousarray(v) for k, v in base.items()}
    x = inputs["L0_attn_a_x"]
    in_maps = []
    for i in range(8):
        m = dict(base)
        m["x"] = np.ascontiguousarray(x[i])
        in_maps.append(m)
    res = run_bass_kernel_spmd(nc, in_maps, core_ids=list(range(8)))
    return np.stack([np.asarray(r["out"]) for r in res.results]).astype(np.float32)


# version 13
# speedup vs baseline: 1.0479x; 1.0479x over previous
import numpy as np
import ml_dtypes
from contextlib import ExitStack
import concourse.bass as bass
import concourse.mybir as mybir
from concourse.bass_utils import run_bass_kernel_spmd

F32 = mybir.dt.float32
BF16 = mybir.dt.bfloat16
AF = mybir.ActivationFunctionType
ALU = mybir.AluOpType
AX = mybir.AxisListType

T, C = 2048, 1024
NT, NCT = 16, 8
NE = 8
SHIFT = 4.0
NEG = -1.0e9
EPS = 1e-5

ENGS = ("tensor", "scalar", "vector", "sync", "gpsimd")
SEM_OF = {"tensor": "Tm", "scalar": "Sc", "vector": "Ve", "sync": "Ls", "gpsimd": "Lg"}

P_QT, P_KT, P_VAUG, P_U = 0, 2048, 4096, 6176
P_WN, P_WT, P_MASK = 14368, 20512, 26656
P_Q1 = 28704
POOL_N = 30752
ATTN_POOL_KEYS = (["qT", "kT", "vaug"] + [f"U:{i}" for i in range(16)]
                  + ["wn:0", "wn:1", "wt:0", "wt:1"])
MOE_POOL_KEYS = ["wi:0a", "wi:0b", "wi:1", "wo_e", "rw"]


def btiles(j):
    out = []
    for i in range(16):
        d = 256 + j * 512 - i * 128
        if d >= 128:
            out.append((i, None))
        elif d > -512:
            out.append((i, (-d) // 128))
    return out


class Sched:
    def __init__(self):
        self.prog = {e: [] for e in ENGS}
        self.cnt = {s: 0 for s in ("Tm", "Sc", "Ve", "Ou")}
        self.hwm = {e: {} for e in ENGS}
        self.lw = {}
        self.lu = {}
        self.prev_w = {"vector": (frozenset(), 0), "scalar": (frozenset(), 0)}

    def step(self, eng, emit, reads=(), writes=(), dma=False, out_sem=None):
        sem = out_sem or SEM_OF[eng]
        own = SEM_OF[eng]
        amt = 16 if dma else 1
        if sem not in self.cnt:
            self.cnt[sem] = 0
        want = {}

        emit_own = eng in ("vector", "scalar")

        def add(t):
            if t is None:
                return
            s, v = t
            if s == own and not emit_own:
                return
            want[s] = max(want.get(s, 0), v)

        for k in reads:
            add(self.lw.get(k))
        for k in writes:
            add(self.lw.get(k))
            for s, v in self.lu.get(k, {}).items():
                if s == own and not emit_own:
                    continue
                want[s] = max(want.get(s, 0), v)
        waits = []
        for s, v in want.items():
            if self.hwm[eng].get(s, 0) < v:
                waits.append((s, v))
                self.hwm[eng][s] = v
        need_drain = False
        self_wait = 0
        if eng in ("vector", "scalar"):
            pw, pv = self.prev_w[eng]
            if (set(reads) | set(writes)) & pw:
                self_wait = pv
                need_drain = (eng == "vector")
        self.cnt[sem] += amt
        val = self.cnt[sem]
        if eng in ("vector", "scalar"):
            self.prev_w[eng] = (frozenset(writes), self.cnt[SEM_OF[eng]])
        self.prog[eng].append((tuple(waits), emit, sem, val, need_drain, self_wait))
        for k in writes:
            self.lw[k] = (sem, val)
            self.lu.pop(k, None)
        for k in reads:
            self.lu.setdefault(k, {})[sem] = val
        return val


class MK:
    def __init__(self, n_sub=4, taps=()):
        self.n_sub = n_sub
        self.taps_req = list(taps)
        nc = bass.Bass("TRN2", target_bir_lowering=False)
        self.nc = nc
        self.sc = Sched()
        d = nc.dram_tensor
        self.inp = {}
        for name, shape in [
            ("x", [T, C]),
            ("aqw", [C, C]), ("aqb", [C]), ("akw", [C, C]), ("akb", [C]),
            ("avw", [C, C]), ("avb", [C]), ("aow", [C, C]), ("aob", [C]),
            ("ipw", [3 * C, C]), ("ipb", [3 * C]), ("opw", [C, C]), ("opb", [C]),
            ("rw", [C, NE]), ("win", [NE, C, C]), ("bin", [NE, C]),
            ("wout", [NE, C, C]), ("bout", [NE, C]),
        ]:
            self.inp[name] = d(name, shape, F32, kind="ExternalInput").ap()
        self.inp["ident"] = d("ident", [128, 128], BF16, kind="ExternalInput").ap()
        self.inp["bmask"] = d("bmask", [128, 2048], BF16, kind="ExternalInput").ap()
        self.out = d("out", [T, C], F32, kind="ExternalOutput").ap()
        self.cur_set = None

    def T_(self, emit, **kw):
        return self.sc.step("tensor", emit, **kw)

    def S_(self, emit, **kw):
        return self.sc.step("scalar", emit, **kw)

    def V_(self, emit, **kw):
        return self.sc.step("vector", emit, **kw)

    def DMA(self, queue, emit, dsem="d0", **kw):
        return self.sc.step(queue, emit, dma=True, out_sem=dsem, **kw)

    def act_set(self, func):
        s = {AF.Exp: "exp", AF.Ln: "exp", AF.Gelu_apprx_tanh: "gelu"}.get(func)
        if s is not None and s != self.cur_set:
            self.cur_set = s
            from concourse.hw_specs import get_activation_tables
            names = list(get_activation_tables(self.nc.m.arch).keys())
            idx = names.index({"exp": "natural_log_exp_and_others",
                               "gelu": "gelu_apprx_tanh_and_others"}[s])

            def emit(e, idx=idx):
                e.add_instruction(mybir.InstLoadActFuncSet(
                    name=e.bass.get_next_instruction_name(),
                    act_func_set_id=idx, ins=[], outs=[]))
                return e.drain()
            self.S_(emit)

    def alloc(self, ctx):
        nc = self.nc
        E = ctx.enter_context
        B = {}
        B["x"] = E(nc.sbuf_tensor("xres", [128, NT * 1024], F32))
        B["xlt"] = E(nc.sbuf_tensor("xlt", [128, NCT * 2048], BF16))
        B["big"] = E(nc.sbuf_tensor("big", [128, NCT * 2048], BF16))
        B["pool"] = E(nc.sbuf_tensor("pool", [128, POOL_N], BF16))
        B["lnrow"] = E(nc.sbuf_tensor("lnrow", [128, 2 * 1024], BF16))
        B["ident"] = E(nc.sbuf_tensor("identsb", [128, 128], BF16))
        B["ones"] = E(nc.sbuf_tensor("ones", [1, 512], BF16))
        B["st"] = E(nc.sbuf_tensor("st", [128, 64], F32))
        B["kmT"] = E(nc.sbuf_tensor("kmT", [128, 16], F32))
        B["bcol"] = E(nc.sbuf_tensor("bcol", [128, 24], BF16))
        B["bincol"] = E(nc.sbuf_tensor("bincol", [128, 64], BF16))
        B["bstage"] = E(nc.sbuf_tensor("bstage", [8, 1024], BF16))
        B["obias"] = E(nc.sbuf_tensor("obias", [1, 1024], BF16))
        B["probs"] = E(nc.sbuf_tensor("probs", [128, 128], F32))
        B["gates"] = E(nc.sbuf_tensor("gates", [128, 128], F32))
        B["r16"] = E(nc.sbuf_tensor("r16", [128, 32], F32))
        B["mx8"] = E(nc.sbuf_tensor("mx8", [128, 8], F32))
        B["rinv"] = E(nc.sbuf_tensor("rinv", [1, 512], BF16))
        B["rln"] = E(nc.sbuf_tensor("rln", [1, 512], BF16))
        B["consts"] = E(nc.sbuf_tensor("consts", [128, 2], F32))
        B["bcs"] = E(nc.sbuf_tensor("bcs", [64, 512], BF16))
        B["oplb"] = E(nc.sbuf_tensor("oplb", [128, 1024], BF16))
        B["psA"] = E(nc.psum_tensor("psA", [128, 1024], F32))
        B["psB"] = E(nc.psum_tensor("psB", [128, 1024], F32))
        B["psC"] = E(nc.psum_tensor("psC", [128, 512], F32))
        B["psD"] = E(nc.psum_tensor("psD", [128, 512], F32))
        B["psE"] = E(nc.psum_tensor("psE", [128, 512], F32))
        B["psT"] = E(nc.psum_tensor("psT", [128, 1024], BF16))
        self.B = B
        self.xv = B["x"][:, :].rearrange("p (i c) -> p i c", i=NT)
        self.xltv = B["xlt"][:, :].rearrange("p (k t) -> p k t", k=NCT)
        self.bigv = B["big"][:, :].rearrange("p (k t) -> p k t", k=NCT)

    def pool(self, lo, hi):
        return self.B["pool"][:, lo:hi]

    def prologue(self):
        B = self.B

        def e_x(e):
            return e.dma_start(out=self.xv,
                               in_=self.inp["x"].rearrange("(i p) c -> p i c", p=128))
        self.DMA("sync", e_x, dsem="dx", writes=[f"x:{i}" for i in range(NT)])

        self.DMA("sync", lambda e: e.dma_start(out=B["ident"][:, :], in_=self.inp["ident"]),
                 dsem="dident", writes=["ident"])
        self.V_(lambda e: e.memset(B["ones"][0:1, :], 1.0), writes=["ones"])
        self.V_(lambda e: e.memset(B["consts"][:, 0:1], EPS), writes=["consts"])
        self.V_(lambda e: e.memset(B["consts"][:, 1:2], -SHIFT), writes=["consts"])

    def ln_phase(self, kmask):
        B = self.B
        st = B["st"]
        for i in range(NT):
            sl = (i % 4) * 16
            xt = self.xv[:, i, :]
            def e_bn(e, xt=xt, sl=sl):
                e.bn_stats(st[:, sl + 4:sl + 10], xt[:, 0:512])
                return e.bn_stats(st[:, sl + 10:sl + 16], xt[:, 512:1024])
            self.V_(e_bn, reads=[f"x:{i}"], writes=[f"st:{i % 4}"])

            def e_ag(e, sl=sl):
                return e.bn_aggr(st[:, sl + 0:sl + 2], st[:, sl + 4:sl + 16])
            self.V_(e_ag, reads=[f"st:{i % 4}"], writes=[f"st:{i % 4}"])
            self.act_set(AF.Ln)

            def e_ln1(e, sl=sl):
                return e.activation(st[:, sl + 2:sl + 3], st[:, sl + 1:sl + 2], AF.Ln,
                                    bias=self.B["consts"][:, 0:1])
            self.S_(e_ln1, reads=[f"st:{i % 4}", "consts"], writes=[f"st:{i % 4}"])

            def e_ln2(e, sl=sl):
                return e.activation(st[:, sl + 3:sl + 4], st[:, sl + 2:sl + 3],
                                    AF.Exp, scale=-0.5)
            self.S_(e_ln2, reads=[f"st:{i % 4}"], writes=[f"st:{i % 4}"])
            def e_nm(e, sl=sl):
                return e.tensor_scalar(st[:, sl + 0:sl + 1], st[:, sl + 0:sl + 1],
                                       st[:, sl + 3:sl + 4], -1.0,
                                       op0=ALU.mult, op1=ALU.mult)
            self.V_(e_nm, reads=[f"st:{i % 4}"], writes=[f"st:{i % 4}"])
            lr = B["lnrow"][:, (i % 2) * 1024:(i % 2) * 1024 + 1024]

            def e_ap(e, xt=xt, lr=lr, sl=sl):
                return e.activation(lr, xt, AF.Identity, scale=st[:, sl + 3:sl + 4],
                                    bias=st[:, sl + 0:sl + 1])
            self.S_(e_ap, reads=[f"x:{i}", f"st:{i % 4}"], writes=[f"lnrow:{i % 2}"])
            if kmask:
                def e_km(e, lr=lr, i=i):
                    return e.tensor_scalar(B["kmT"][:, i:i + 1], lr[:, 0:1], 0.0,
                                           None, op0=ALU.not_equal)
                self.V_(e_km, reads=[f"lnrow:{i % 2}"], writes=["kmT"])
            def e_tr(e, lr=lr):
                ins = None
                for k in range(8):
                    ins = e.transpose(B["psT"][:, k * 128:(k + 1) * 128],
                                      lr[:, k * 128:(k + 1) * 128], B["ident"][:, :])
                return ins
            self.T_(e_tr, reads=[f"lnrow:{i % 2}", "ident"], writes=["psT"])

            def e_ev(e, i=i):
                src = B["psT"][:, :].rearrange("p (k t) -> p k t", k=8)
                dst = self.xltv[:, :, i * 128:(i + 1) * 128]
                return e.tensor_copy(dst, src)
            self.V_(e_ev, reads=["psT"], writes=[f"xlt:{i // 4}", "psT"])

    def attn_phase(self, cfg):
        B = self.B
        nc = self.nc
        band = cfg["band"]
        ob_ap = self.inp[cfg["ob"]].rearrange("(o c) -> o c", o=1)
        self.DMA("gpsimd", lambda e, a=ob_ap: e.dma_start(out=B["obias"][0:1, :], in_=a),
                 dsem="dob", writes=["obias"])
        vv = self.pool(P_VAUG, P_VAUG + 16 * 130).rearrange("p (t s) -> p t s", t=16)
        wx = ["wi:0a"] if cfg.get("fence_moe") else []
        self.V_(lambda e, v=vv: e.memset(v[:, :, 64:65], 1.0), writes=["vaug"] + wx)
        self.V_(lambda e, v=vv: e.memset(v[:, :, 129:130], 1.0), writes=["vaug"])
        q0 = self.pool(P_QT, P_QT + 2048)
        q1 = self.pool(P_Q1, P_Q1 + 2048)
        self.V_(lambda e, a=q0: e.memset(a[64:128, :], 0.0), writes=["qT"])
        self.V_(lambda e, a=q1: e.memset(a[0:64, :], 0.0), writes=["qT"])
        if cfg["band"]:
            mask_ap = self.pool(P_MASK, P_MASK + 2048)
            self.DMA("sync", lambda e, m=mask_ap: e.dma_start(out=m, in_=self.inp["bmask"]),
                     dsem="dmask", writes=["mask"])
        self.bias_cols_all(cfg)
        for p in range(9):
            if p < 8:
                self.wn_load(cfg, p)
            if p == 1:
                self.wt_transpose(0)
            if p > 0:
                self.qkv_pair(cfg, p - 1)
                self.core_pair(cfg, p - 1, band,
                               host_wt=(p if p < 8 else None))
        self.o_proj(cfg)

    def wn_load(self, cfg, p):
        b = p % 2
        fence = list(MOE_POOL_KEYS) if (p == 0 and cfg.get("fence_moe")) else []
        dst = self.pool(P_WN + b * 3072, P_WN + b * 3072 + 3072)
        dstv = dst.rearrange("p (w c) -> p w c", w=3)
        for w, src in enumerate(cfg["qkv_w"]):
            ap = src[p * 128:(p + 1) * 128, :]

            def e(e_, d=dstv[:, w, :], a=ap):
                return e_.dma_start(out=d, in_=a)
            self.DMA("gpsimd", e, dsem=f"dwn{b}", writes=[f"wn:{b}"] + fence)
            fence = []

    def bias_cols_all(self, cfg):
        B = self.B
        for w, srcb in enumerate(cfg["qkv_b"]):
            ap = srcb.rearrange("(k p) -> k p", k=8)

            def e(e_, d=B["bstage"][0:8, w * 128:(w + 1) * 128], a=ap):
                return e_.dma_start(out=d, in_=a)
            self.DMA("gpsimd", e, dsem="dbst0", writes=["bst"])

        def e_tr(e):
            ins = None
            for w in range(3):
                ins = e.transpose(B["psT"][:, w * 8:(w + 1) * 8],
                                  B["bstage"][0:8, w * 128:(w + 1) * 128],
                                  B["ident"][0:8, 0:8])
            return ins
        self.T_(e_tr, reads=["bst", "ident"], writes=["psT"])

        def e_ev(e):
            return e.tensor_copy(B["bcol"][:, 0:24], B["psT"][:, 0:24])
        self.V_(e_ev, reads=["psT"], writes=["bcol", "psT"])

    def wt_transpose(self, p, only_w=None):
        B = self.B
        b = p % 2
        src = self.pool(P_WN + b * 3072, P_WN + b * 3072 + 3072)
        srcv = src.rearrange("p (w c) -> p w c", w=3)
        dst = self.pool(P_WT + b * 3072, P_WT + b * 3072 + 3072)
        dstv = dst.rearrange("p (w k d) -> p w k d", w=3, k=8)
        for w in (range(3) if only_w is None else [only_w]):
            def e_tr(e, w=w, b=b):
                ins = None
                for k in range(8):
                    ins = e.transpose(B["psT"][:, k * 128:(k + 1) * 128],
                                      srcv[:, w, k * 128:(k + 1) * 128],
                                      B["ident"][:, :])
                return ins
            self.T_(e_tr, reads=[f"wn:{b}", "ident"], writes=["psT"])

            def e_ev(e, w=w):
                return e.tensor_copy(dstv[:, w, :, :],
                                     B["psT"][:, :].rearrange("p (k d) -> p k d", k=8))
            self.V_(e_ev, reads=["psT"], writes=[f"wt:{b}", "psT"])

    def qkv_pair(self, cfg, p):
        B = self.B
        b = p % 2
        wtv = self.pool(P_WT + b * 3072, P_WT + b * 3072 + 3072).rearrange(
            "p (w k d) -> p w k d", w=3, k=8)
        q0 = self.pool(P_QT, P_QT + 2048)
        q1 = self.pool(P_Q1, P_Q1 + 2048)
        kT = self.pool(P_KT, P_KT + 2048)
        ureg = self.pool(P_U, P_U + 8192)
        dests = [("qT", None), ("kT", kT), ("vT", ureg[:, 0:2048])]
        for w, (dkey, dap) in enumerate(dests):
            for half in range(2):
                ps = B["psA"] if half == 0 else B["psB"]
                pskeys = ["psA0", "psA1"] if half == 0 else ["psB0", "psB1"]
                for cc in range(2):
                    chunk = half * 2 + cc

                    def e_mm(e, w=w, cc=cc, chunk=chunk, ps=ps, b=b):
                        ins = None
                        for k in range(8):
                            ins = e.matmul(ps[:, cc * 512:(cc + 1) * 512],
                                           wtv[:, w, k, :],
                                           self.xltv[:, k, chunk * 512:(chunk + 1) * 512],
                                           start=(k == 0), stop=(k == 7))
                        return ins
                    self.T_(e_mm, reads=[f"wt:{b}", f"xlt:{chunk}"],
                            writes=[pskeys[cc]])
                wkeys = [dkey] if dkey != "vT" else ["U:0", "U:1", "U:2", "U:3"]

                bc = w * 8 + p

                def e_ev(e, dap=dap, half=half, ps=ps, bc=bc):
                    if dap is None:
                        e.activation(q0[0:64, half * 1024:(half + 1) * 1024],
                                     ps[0:64, :], AF.Identity,
                                     bias=B["bcol"][0:64, bc:bc + 1])
                        return e.activation(q1[64:128, half * 1024:(half + 1) * 1024],
                                            ps[64:128, :], AF.Identity,
                                            bias=B["bcol"][64:128, bc:bc + 1])
                    return e.activation(dap[:, half * 1024:(half + 1) * 1024],
                                        ps[:, :], AF.Identity,
                                        bias=B["bcol"][:, bc:bc + 1])
                self.S_(e_ev, reads=pskeys + ["bcol"], writes=wkeys + pskeys)
        vv = self.pool(P_VAUG, P_VAUG + 16 * 130).rearrange("p (t s) -> p t s", t=16)
        for half in range(2):
            def e_tr(e, half=half):
                ins = None
                for k in range(8):
                    ins = e.transpose(B["psT"][:, k * 128:(k + 1) * 128],
                                      ureg[:, half * 1024 + k * 128:half * 1024 + (k + 1) * 128],
                                      B["ident"][:, :])
                return ins
            self.T_(e_tr, reads=["U:0", "U:1", "U:2", "U:3", "ident"], writes=["psT"])

            def e_ev(e, half=half):
                src = B["psT"][:, :].rearrange("p (t g x) -> p t g x", t=8, g=2)
                dst = vv[:, half * 8:(half + 1) * 8, :].rearrange(
                    "p t (g y) -> p t g y", g=2)[:, :, :, 0:64]
                return e.tensor_copy(dst, src)
            self.V_(e_ev, reads=["psT"], writes=["vaug", "psT"])

    def core_pair(self, cfg, p, band, host_wt=None):
        B = self.B
        qbufs = (self.pool(P_QT, P_QT + 2048), self.pool(P_Q1, P_Q1 + 2048))
        kT = self.pool(P_KT, P_KT + 2048)
        ureg = self.pool(P_U, P_U + 8192)
        uv = ureg.rearrange("p (h s) -> p h s", h=2)
        vv = self.pool(P_VAUG, P_VAUG + 16 * 130).rearrange("p (t s) -> p t s", t=16)
        maskv = self.pool(P_MASK, P_MASK + 2048)
        self.act_set(AF.Exp)
        R = 8
        for j in range(4):
            if host_wt is not None and j >= 1:
                self.wt_transpose(host_wt, only_w=j - 1)
            tiles = btiles(j) if band else [(i, None) for i in range(16)]
            n = len(tiles)

            def ukeys(g):
                return [f"U:{g % R}", f"U:{8 + g % R}"]

            def emit_scores(g):
                i, mi = tiles[g]
                ps = B["psA"] if g % 2 == 0 else B["psB"]
                pk = "psA" if g % 2 == 0 else "psB"
                sl = (g % R) * 512

                def e_mm(e, i=i, j=j, ps=ps):
                    e.matmul(ps[:, 0:512], kT[:, i * 128:(i + 1) * 128],
                             qbufs[0][:, j * 512:(j + 1) * 512],
                             start=True, stop=True)
                    return e.matmul(ps[:, 512:1024], kT[:, i * 128:(i + 1) * 128],
                                    qbufs[1][:, j * 512:(j + 1) * 512],
                                    start=True, stop=True)
                self.T_(e_mm, reads=["qT", "kT"], writes=[pk + "0", pk + "1"])
                uout = uv[:, :, sl:sl + 512]

                def e_ex(e, ps=ps, uout=uout):
                    return e.activation(uout, ps[:, :], AF.Exp, scale=0.125,
                                        bias=self.B["consts"][:, 1:2])
                self.S_(e_ex, reads=[pk + "0", pk + "1", "consts"],
                        writes=ukeys(g) + [pk + "0", pk + "1"])
                if not band:
                    def e_mk(e, uout=uout, i=i):
                        return e.tensor_scalar(uout, uout, B["kmT"][:, i:i + 1],
                                               None, op0=ALU.mult)
                    self.V_(e_mk, reads=ukeys(g) + ["kmT"], writes=ukeys(g))
                elif mi is not None:
                    def e_mkb(e, uout=uout, mi=mi):
                        e.tensor_tensor(uout[:, 0, :], uout[:, 0, :],
                                        maskv[:, mi * 512:(mi + 1) * 512],
                                        op=ALU.mult)
                        return e.tensor_tensor(uout[:, 1, :], uout[:, 1, :],
                                               maskv[:, mi * 512:(mi + 1) * 512],
                                               op=ALU.mult)
                    self.V_(e_mkb, reads=ukeys(g) + ["mask"], writes=ukeys(g))

            def emit_pv(g):
                i, mi = tiles[g]
                sl = (g % R) * 512

                def e_pv(e, g=g, i=i, sl=sl, n=n):
                    e.matmul(B["psC"][0:65, :], vv[:, i, 0:65],
                             uv[:, 0, sl:sl + 512],
                             start=(g == 0), stop=(g == n - 1))
                    return e.matmul(B["psD"][0:65, :], vv[:, i, 65:130],
                                    uv[:, 1, sl:sl + 512],
                                    start=(g == 0), stop=(g == n - 1))
                self.T_(e_pv, reads=["vaug"] + ukeys(g), writes=["psC", "psD"])

            for g in range(n):
                emit_scores(g)
                if g >= 2:
                    emit_pv(g - 2)
            emit_pv(n - 2)
            emit_pv(n - 1)
            for h, pvk in ((0, "psC"), (1, "psD")):
                pv = B[pvk]

                def e_ri1(e, pv=pv):
                    return e.activation(B["rln"][0:1, :], pv[64:65, :], AF.Ln)
                self.S_(e_ri1, reads=[pvk], writes=["rln"])

                def e_ri2(e):
                    with self.nc.allow_low_precision(
                            reason="bf16 softmax denominator for bf16 PV product"):
                        return e.activation(B["rinv"][0:1, :], B["rln"][0:1, :],
                                            AF.Exp, scale=-1.0)
                self.S_(e_ri2, reads=["rln"], writes=["rinv"])

                def e_bc(e):
                    return e.matmul(B["psE"][0:64, :], B["ones"][0:1, 0:64],
                                    B["rinv"][0:1, :], start=True, stop=True)
                self.T_(e_bc, reads=["rinv", "ones"], writes=["psE"])

                def e_cp(e):
                    return e.tensor_copy(B["bcs"][0:64, :], B["psE"][0:64, :])
                self.V_(e_cp, reads=["psE"], writes=["bcs", "psE"])

                def e_mu(e, pv=pv, p=p, h=h, j=j):
                    dst = self.bigv[h * 64:(h + 1) * 64, p, j * 512:(j + 1) * 512]
                    return e.tensor_tensor(dst, pv[0:64, :], B["bcs"][0:64, :],
                                           op=ALU.mult)
                self.V_(e_mu, reads=[pvk, "bcs"],
                        writes=[f"big:{4 * j + q}" for q in range(4)] + [pvk])

    def o_proj(self, cfg):
        B = self.B
        wof = cfg["ow"]
        woT = self.pool(P_U, P_U + 8192).rearrange("p (k d) -> p k d", k=8)
        ukeys = [f"U:{i}" for i in range(16)]
        for s in range(8):
            b = s % 2
            dst = self.pool(P_WN + b * 3072, P_WN + b * 3072 + 1024)

            def e_dma(e, dst=dst, s=s):
                return e.dma_start(out=dst, in_=wof[s * 128:(s + 1) * 128, :])
            self.DMA("gpsimd", e_dma, dsem=f"dwn{b}", writes=[f"wn:{b}"])

            def e_tr(e, dst=dst):
                ins = None
                for k in range(8):
                    ins = e.transpose(B["psT"][:, k * 128:(k + 1) * 128],
                                      dst[:, k * 128:(k + 1) * 128], B["ident"][:, :])
                return ins
            self.T_(e_tr, reads=[f"wn:{b}", "ident"], writes=["psT"])

            def e_ev(e, s=s):
                src = B["psT"][:, :].rearrange("p (k d) -> p k d", k=8)
                return e.tensor_copy(woT[:, :, s * 128:(s + 1) * 128], src)
            self.V_(e_ev, reads=["psT"],
                    writes=["psT", "woT"] + (ukeys if s == 0 else []))
        opl = B["oplb"][:, :]
        for db in range(2):
            def e_bc(e, db=db):
                return e.matmul(B["psE"][:, :], B["ones"][0:1, 0:128],
                                B["obias"][0:1, db * 512:(db + 1) * 512],
                                start=True, stop=True)
            self.T_(e_bc, reads=["obias", "ones"], writes=["psE"])

            def e_cp(e, db=db):
                return e.tensor_copy(opl[:, db * 512:(db + 1) * 512], B["psE"][:, :])
            self.V_(e_cp, reads=["psE"], writes=["oplb", "psE"])
        for tt in range(NT):
            ps = B["psA"] if tt % 2 == 0 else B["psB"]
            pk = "psA" if tt % 2 == 0 else "psB"
            for db in range(2):
                def e_mm(e, tt=tt, db=db, ps=ps):
                    ins = None
                    for k in range(8):
                        ins = e.matmul(ps[:, db * 512:(db + 1) * 512],
                                       self.bigv[:, k, tt * 128:(tt + 1) * 128],
                                       woT[:, k, db * 512:(db + 1) * 512],
                                       start=(k == 0), stop=(k == 7))
                    return ins
                self.T_(e_mm, reads=["woT", f"big:{tt}"]
                        + [f"U:{i}" for i in range(16)],
                        writes=[pk + str(db)])

            def e_add1(e, tt=tt, ps=ps):
                return e.tensor_tensor(self.xv[:, tt, :], self.xv[:, tt, :],
                                       ps[:, :], op=ALU.add)
            self.V_(e_add1, reads=[pk + "0", pk + "1", f"x:{tt}"],
                    writes=[f"x:{tt}", pk + "0", pk + "1"])

            def e_add2(e, tt=tt):
                return e.tensor_tensor(self.xv[:, tt, :], self.xv[:, tt, :],
                                       opl[:, :], op=ALU.add)
            self.V_(e_add2, reads=[f"x:{tt}", "oplb"], writes=[f"x:{tt}"])

    def moe_phase(self, first):
        B = self.B
        wi0a = self.pool(0, 6144).rearrange("p (k h) -> p k h", k=6)
        wi0b = self.pool(P_Q1, P_Q1 + 2048).rearrange("p (k h) -> p k h", k=2)
        wi1 = self.pool(8192, 16384).rearrange("p (k h) -> p k h", k=8)

        def wi_tile(b, k, mt):
            if b == 1:
                return wi1[:, k, mt * 128:(mt + 1) * 128]
            if k < 6:
                return wi0a[:, k, mt * 128:(mt + 1) * 128]
            return wi0b[:, k - 6, mt * 128:(mt + 1) * 128]
        wov = self.pool(16384, 16384 + 8192).rearrange("p (k d) -> p k d", k=8)
        rwv = self.pool(24576, 24576 + 64).rearrange("p (k e) -> p k e", k=8)
        fence = list(ATTN_POOL_KEYS)

        def dma_wi(e_idx):
            b = e_idx % 2
            nonlocal fence
            if b == 1:
                def e(e_, e_idx=e_idx):
                    return e_.dma_start(out=wi1[:, :, :],
                                        in_=self.inp["win"][e_idx:e_idx + 1, :, :]
                                        .rearrange("o (k p) h -> p (o k) h", p=128))
                self.DMA("gpsimd", e, dsem="dwi1", writes=["wi:1"] + fence)
            else:
                def ea(e_, e_idx=e_idx):
                    return e_.dma_start(out=wi0a[:, :, :],
                                        in_=self.inp["win"][e_idx:e_idx + 1, 0:768, :]
                                        .rearrange("o (k p) h -> p (o k) h", p=128))
                self.DMA("gpsimd", ea, dsem="dwi0",
                         writes=["wi:0a", "qT", "kT", "vaug"] + fence)

                def eb(e_, e_idx=e_idx):
                    return e_.dma_start(out=wi0b[:, :, :],
                                        in_=self.inp["win"][e_idx:e_idx + 1, 768:1024, :]
                                        .rearrange("o (k p) h -> p (o k) h", p=128))
                self.DMA("gpsimd", eb, dsem="dwi0b", writes=["wi:0b"])
            fence = []

        def bias_all():
            def e_bi(e_):
                return e_.dma_start(
                    out=B["bstage"][0:8, :].rearrange("k (e p) -> k e p", e=8),
                    in_=self.inp["bin"].rearrange("e (k p) -> k e p", k=8))
            self.DMA("gpsimd", e_bi, dsem="dbst1", writes=["bst"])

            def e_tr(e_):
                ins = None
                for ei in range(8):
                    ins = e_.transpose(B["psT"][:, ei * 8:(ei + 1) * 8],
                                       B["bstage"][0:8, ei * 128:(ei + 1) * 128],
                                       B["ident"][0:8, 0:8])
                return ins
            self.T_(e_tr, reads=["bst", "ident"], writes=["psT"])

            def e_ev(e_):
                return e_.tensor_copy(B["bincol"][:, 0:64], B["psT"][:, 0:64])
            self.V_(e_ev, reads=["psT"], writes=["binc", "psT"])


        def e_rw(e_):
            return e_.dma_start(out=rwv[:, :, :],
                                in_=self.inp["rw"].rearrange("(k p) e -> p k e", p=128))
        self.DMA("gpsimd", e_rw, dsem="drw", writes=["rw"] + fence)
        fence = []
        dma_wi(0)
        bias_all()
        for tt in range(NT):
            def e_mm(e, tt=tt):
                ins = None
                for k in range(8):
                    ins = e.matmul(B["psC"][:, tt * 8:(tt + 1) * 8],
                                   self.xltv[:, k, tt * 128:(tt + 1) * 128],
                                   rwv[:, k, :], start=(k == 0), stop=(k == 7))
                return ins
            self.T_(e_mm, reads=["rw", f"xlt:{tt // 4}"], writes=["psC"])
        self.act_set(AF.Exp)
        self.S_(lambda e: e.activation(B["probs"][:, :], B["psC"][:, 0:128], AF.Exp),
                reads=["psC"], writes=["probs", "psC"])
        pv = B["probs"][:, :].rearrange("p (t e) -> p t e", t=16)
        self.V_(lambda e: e.reduce_sum(B["r16"][:, 0:16], pv, AX.X),
                reads=["probs"], writes=["r16"])
        self.V_(lambda e: e.reciprocal(B["r16"][:, 16:32], B["r16"][:, 0:16]),
                reads=["r16"], writes=["r16"])
        gT = B["pool"][0:8, P_MASK:P_MASK + 2048]
        boall = B["pool"][0:8, 24640:25664]
        gbf = B["lnrow"][:, 1024:1152]

        def e_ba(e_):
            return e_.dma_start(out=boall, in_=self.inp["bout"])
        self.DMA("gpsimd", e_ba, dsem="dbo", writes=["wt:1"])
        for tt in range(NT):
            def e_g1(e, tt=tt):
                return e.max(out=B["mx8"][:, :], in_=B["probs"][:, tt * 8:(tt + 1) * 8])
            self.V_(e_g1, reads=["probs"], writes=["mx8"])

            def e_g2(e, tt=tt):
                return e.scalar_tensor_tensor(
                    B["gates"][:, tt * 8:(tt + 1) * 8],
                    B["probs"][:, tt * 8:(tt + 1) * 8], B["mx8"][:, 1:2],
                    B["probs"][:, tt * 8:(tt + 1) * 8], op0=ALU.is_ge, op1=ALU.mult)
            self.V_(e_g2, reads=["probs", "mx8"], writes=["gates"])

            def e_g3(e, tt=tt):
                return e.tensor_scalar(B["gates"][:, tt * 8:(tt + 1) * 8],
                                       B["gates"][:, tt * 8:(tt + 1) * 8],
                                       B["r16"][:, 16 + tt:17 + tt], None, op0=ALU.mult)
            self.V_(e_g3, reads=["gates", "r16"], writes=["gates"])
        def build_gT():
            self.V_(lambda e: e.tensor_copy(gbf, B["gates"][:, :]),
                    reads=["gates"], writes=["lnrow:1"])
            for fill in range(2):
                def e_tr(e, fill=fill):
                    ins = None
                    for i in range(8):
                        tt = fill * 8 + i
                        ins = e.transpose(B["psT"][0:8, i * 128:(i + 1) * 128],
                                          gbf[:, tt * 8:(tt + 1) * 8],
                                          B["ident"][:, :])
                    return ins
                self.T_(e_tr, reads=["lnrow:1", "ident"], writes=["psT"])

                def e_ev(e, fill=fill):
                    return e.tensor_copy(gT[0:8, fill * 1024:(fill + 1) * 1024],
                                         B["psT"][0:8, 0:1024])
                self.V_(e_ev, reads=["psT"], writes=["mask", "psT"])

        for ei in range(NE):
            b = ei % 2
            def e_wo(e_, ei=ei):
                return e_.dma_start(out=wov[:, :, :],
                                    in_=self.inp["wout"][ei:ei + 1, :, :].rearrange(
                                        "o (k p) d -> p (o k) d", p=128))
            self.DMA("gpsimd", e_wo, dsem="dwo", writes=["wo_e"])
            if ei + 1 < NE:
                dma_wi(ei + 1)
            self.act_set(AF.Gelu_apprx_tanh)
            for mt in range(8):
                for half in range(2):
                    ps = B["psA"] if half == 0 else B["psB"]
                    pk = "psA" if half == 0 else "psB"
                    for cc in range(2):
                        chunk = half * 2 + cc

                        def e_mm(e, mt=mt, cc=cc, chunk=chunk, ps=ps, b=b):
                            ins = None
                            for k in range(8):
                                ins = e.matmul(ps[:, cc * 512:(cc + 1) * 512],
                                               wi_tile(b, k, mt),
                                               self.xltv[:, k, chunk * 512:(chunk + 1) * 512],
                                               start=(k == 0), stop=(k == 7))
                            return ins
                        self.T_(e_mm,
                                reads=(["wi:1"] if b == 1 else ["wi:0a", "wi:0b"])
                                + [f"xlt:{chunk}"],
                                writes=[pk + str(cc)])

                    def e_gl(e, mt=mt, half=half, ps=ps, ei=ei):
                        return e.activation(self.bigv[:, mt, half * 1024:(half + 1) * 1024],
                                            ps[:, :], AF.Gelu_apprx_tanh,
                                            bias=B["bincol"][:, ei * 8 + mt:ei * 8 + mt + 1])
                    self.S_(e_gl, reads=[pk + "0", pk + "1", "binc"],
                            writes=[pk + "0", pk + "1"]
                            + [f"big:{half * 8 + q}" for q in range(8)])
            ykeys = ("psC", "psD", "psE")
            for tt in range(NT):
                for db in range(2):
                    yk = ykeys[(tt * 2 + db) % 3]
                    ps = B[yk]

                    def e_mm(e, tt=tt, db=db, ps=ps):
                        ins = None
                        for k in range(8):
                            ins = e.matmul(ps[:, :],
                                           self.bigv[:, k, tt * 128:(tt + 1) * 128],
                                           wov[:, k, db * 512:(db + 1) * 512],
                                           start=(k == 0), stop=(k == 7))
                        return ins
                    self.T_(e_mm, reads=["wo_e", f"big:{tt}"], writes=[yk])

                    def e_acc(e, tt=tt, db=db, ps=ps, ei=ei):
                        dst = self.xv[:, tt, db * 512:(db + 1) * 512]
                        return e.scalar_tensor_tensor(
                            dst, ps[:, :], B["gates"][:, tt * 8 + ei:tt * 8 + ei + 1],
                            dst, op0=ALU.mult, op1=ALU.add)
                    self.V_(e_acc, reads=[yk, "gates", f"x:{tt}"],
                            writes=[f"x:{tt}", yk])
        for tt in range(NT):
            for db in range(2):
                yk = ykeys[(tt * 2 + db) % 3]
                ps = B[yk]

                def e_mm(e, tt=tt, db=db, ps=ps, gT=gT, boall=boall):
                    return e.matmul(ps[:, :], gT[0:8, tt * 128:(tt + 1) * 128],
                                    boall[0:8, db * 512:(db + 1) * 512],
                                    start=True, stop=True)
                self.T_(e_mm, reads=["mask", "wt:1"], writes=[yk])

                def e_add(e, tt=tt, db=db, ps=ps):
                    dst = self.xv[:, tt, db * 512:(db + 1) * 512]
                    return e.tensor_tensor(dst, dst, ps[:, :], op=ALU.add)
                self.V_(e_add, reads=[yk, f"x:{tt}"], writes=[f"x:{tt}", yk])

    def tap_spec(self, name):
        B = self.B
        specs = {
            "st": (B["st"][:, :], [128, 64], F32),
            "kmT": (B["kmT"][:, :], [128, 16], F32),
            "consts": (B["consts"][:, :], [128, 2], F32),
            "xlt0": (B["xlt"][:, 0:2048], [128, 2048], BF16),
            "xlt": (B["xlt"][:, :], [128, NCT * 2048], BF16),
            "lnrow": (B["lnrow"][:, :], [128, 2048], BF16),
            "qT": (self.pool(P_QT, P_QT + 2048), [128, 2048], BF16),
            "kT": (self.pool(P_KT, P_KT + 2048), [128, 2048], BF16),
            "vaug": (self.pool(P_VAUG, P_VAUG + 2080), [128, 2080], BF16),
            "u": (self.pool(P_U, P_U + 8192), [128, 8192], BF16),
            "wt0": (self.pool(P_WT, P_WT + 3072), [128, 3072], BF16),
            "bcol": (B["bcol"][:, :], [128, 8], BF16),
            "big": (B["big"][:, :], [128, NCT * 2048], BF16),
            "probs": (B["probs"][:, :], [128, 128], F32),
            "gates": (B["gates"][:, :], [128, 128], F32),
            "brow": (B["brow"][:, :], [1, 2048], BF16),
            "bincol": (B["bincol"][:, :], [128, 16], BF16),
        }
        return specs[name]

    def epilogue(self):
        def e_out(e):
            return e.dma_start(out=self.out.rearrange("(i p) c -> p i c", p=128),
                               in_=self.xv)
        self.DMA("sync", e_out, dsem="Ou", reads=[f"x:{i}" for i in range(NT)])
        allkeys = list(self.sc.lw.keys())
        for name in self.taps_req:
            ap, shape, dt = self.tap_spec(name)
            o = self.nc.dram_tensor("tap_" + name, shape, dt,
                                    kind="ExternalOutput").ap()

            def e_t(e, o=o, a=ap):
                return e.dma_start(out=o, in_=a)
            self.DMA("sync", e_t, dsem="Ou", reads=allkeys)

    def build(self):
        nc = self.nc
        with ExitStack() as ctx:
            self.alloc(ctx)
            self.prologue()
            cfg_a = dict(band=False, fence_moe=False,
                         qkv_w=[self.inp["aqw"], self.inp["akw"], self.inp["avw"]],
                         qkv_b=[self.inp["aqb"], self.inp["akb"], self.inp["avb"]],
                         ow=self.inp["aow"], ob="aob")
            ip, ipb = self.inp["ipw"], self.inp["ipb"]
            cfg_b = dict(band=True, fence_moe=True,
                         qkv_w=[ip[0:C, :], ip[C:2 * C, :], ip[2 * C:3 * C, :]],
                         qkv_b=[ipb[0:C], ipb[C:2 * C], ipb[2 * C:3 * C]],
                         ow=self.inp["opw"], ob="opb")
            subs = [("attn", cfg_a), ("moe", None), ("attn", cfg_b), ("moe", None)]
            for si, (kind, cfg) in enumerate(subs[: self.n_sub]):
                self.act_set(AF.Ln)
                self.ln_phase(kmask=(kind == "attn" and not (cfg or {}).get("band")))
                if kind == "attn":
                    self.attn_phase(cfg)
                else:
                    self.moe_phase(first=(si == 1))
            self.epilogue()

            S = {}
            for s in self.sc.cnt:
                S[s] = ctx.enter_context(nc.semaphore(s))
            blk = ctx.enter_context(nc.Block())

            def make(eng_name):
                steps = self.sc.prog[eng_name]

                own = SEM_OF[eng_name]

                def body(e):
                    for waits, emit, sem, val, need_drain, self_wait in steps:
                        for s, v in waits:
                            e.wait_ge(S[s], v)
                        if self_wait:
                            e.wait_ge(S[own], self_wait)
                        if need_drain:
                            e.drain()
                        ins = emit(e)
                        if ins is None:
                            ins = e.nop()
                        ins.then_inc(S[sem], 1 if sem in ("Tm", "Sc", "Ve") else 16)
                    if eng_name == "sync":
                        e.wait_ge(S["Ou"], self.sc.cnt["Ou"])
                return body
            blk.tensor(make("tensor"))
            blk.scalar(make("scalar"))
            blk.vector(make("vector"))
            blk.sync(make("sync"))
            blk.gpsimd(make("gpsimd"))
        return nc


def _masks():
    m = np.zeros((128, 2048), dtype=np.float32)
    for mi, d in enumerate((0, -128, -256, -384)):
        for i in range(128):
            m[i, mi * 512:(mi + 1) * 512] = (i <= np.arange(512) + d)
    return m.astype(ml_dtypes.bfloat16)


_CACHE = {}


def _get_nc(n_sub=4, taps=()):
    key = (n_sub, tuple(taps))
    if key not in _CACHE:
        _CACHE[key] = MK(n_sub, taps).build()
    return _CACHE[key]


def kernel(**inputs):
    nc = _get_nc(4)
    base = {
        "aqw": inputs["L0_attn_a_wq_w"], "aqb": inputs["L0_attn_a_wq_b"],
        "akw": inputs["L0_attn_a_wk_w"], "akb": inputs["L0_attn_a_wk_b"],
        "avw": inputs["L0_attn_a_wv_w"], "avb": inputs["L0_attn_a_wv_b"],
        "aow": inputs["L0_attn_a_wo_w"], "aob": inputs["L0_attn_a_wo_b"],
        "ipw": inputs["L0_attn_b_in_proj_w"], "ipb": inputs["L0_attn_b_in_proj_b"],
        "opw": inputs["L0_attn_b_out_proj_w"], "opb": inputs["L0_attn_b_out_proj_b"],
        "rw": inputs["L0_ffn_router_w"], "win": inputs["L0_ffn_w_in"],
        "bin": inputs["L0_ffn_b_in"], "wout": inputs["L0_ffn_w_out"],
        "bout": inputs["L0_ffn_b_out"],
        "ident": np.eye(128, dtype=np.float32).astype(ml_dtypes.bfloat16),
        "bmask": _masks(),
    }
    base = {k: np.ascontiguousarray(v) for k, v in base.items()}
    x = inputs["L0_attn_a_x"]
    in_maps = []
    for i in range(8):
        m = dict(base)
        m["x"] = np.ascontiguousarray(x[i])
        in_maps.append(m)
    res = run_bass_kernel_spmd(nc, in_maps, core_ids=list(range(8)))
    return np.stack([np.asarray(r["out"]) for r in res.results]).astype(np.float32)


# version 15
# speedup vs baseline: 1.1122x; 1.0613x over previous
import numpy as np
import ml_dtypes
from contextlib import ExitStack
import concourse.bass as bass
import concourse.mybir as mybir
from concourse.bass_utils import run_bass_kernel_spmd

F32 = mybir.dt.float32
BF16 = mybir.dt.bfloat16
AF = mybir.ActivationFunctionType
ALU = mybir.AluOpType
AX = mybir.AxisListType

T, C = 2048, 1024
NT, NCT = 16, 8
NE = 8
SHIFT = 4.0
NEG = -1.0e9
EPS = 1e-5

ENGS = ("tensor", "scalar", "vector", "sync", "gpsimd")
SEM_OF = {"tensor": "Tm", "scalar": "Sc", "vector": "Ve", "sync": "Ls", "gpsimd": "Lg"}

P_QT, P_KT, P_VAUG, P_U = 0, 2048, 4096, 6176
P_WN, P_WT, P_MASK = 14368, 20512, 26656
P_Q1 = 28704
POOL_N = 30752
ATTN_POOL_KEYS = (["qT", "kT", "vaug"] + [f"U:{i}" for i in range(16)]
                  + ["wn:0", "wn:1", "wt:0", "wt:1"])
MOE_POOL_KEYS = ["wi:0a", "wi:0b", "wi:1", "wo_e", "rw"]


def btiles(j):
    out = []
    for i in range(16):
        d = 256 + j * 512 - i * 128
        if d >= 128:
            out.append((i, None))
        elif d > -512:
            out.append((i, (-d) // 128))
    return out


class Sched:
    def __init__(self):
        self.prog = {e: [] for e in ENGS}
        self.cnt = {s: 0 for s in ("Tm", "Sc", "Ve", "Ou")}
        self.hwm = {e: {} for e in ENGS}
        self.lw = {}
        self.lu = {}
        self.prev_w = {"vector": (frozenset(), 0), "scalar": (frozenset(), 0)}

    def step(self, eng, emit, reads=(), writes=(), dma=False, out_sem=None):
        sem = out_sem or SEM_OF[eng]
        own = SEM_OF[eng]
        amt = 16 if dma else 1
        if sem not in self.cnt:
            self.cnt[sem] = 0
        want = {}

        emit_own = eng in ("vector", "scalar")

        def add(t):
            if t is None:
                return
            s, v = t
            if s == own and not emit_own:
                return
            want[s] = max(want.get(s, 0), v)

        for k in reads:
            add(self.lw.get(k))
        for k in writes:
            add(self.lw.get(k))
            for s, v in self.lu.get(k, {}).items():
                if s == own and not emit_own:
                    continue
                want[s] = max(want.get(s, 0), v)
        waits = []
        for s, v in want.items():
            if self.hwm[eng].get(s, 0) < v:
                waits.append((s, v))
                self.hwm[eng][s] = v
        need_drain = False
        self_wait = 0
        if eng in ("vector", "scalar"):
            pw, pv = self.prev_w[eng]
            if (set(reads) | set(writes)) & pw:
                self_wait = pv
                need_drain = (eng == "vector")
        self.cnt[sem] += amt
        val = self.cnt[sem]
        if eng in ("vector", "scalar"):
            self.prev_w[eng] = (frozenset(writes), self.cnt[SEM_OF[eng]])
        self.prog[eng].append((tuple(waits), emit, sem, val, need_drain, self_wait))
        for k in writes:
            self.lw[k] = (sem, val)
            self.lu.pop(k, None)
        for k in reads:
            self.lu.setdefault(k, {})[sem] = val
        return val


class MK:
    def __init__(self, n_sub=4, taps=()):
        self.n_sub = n_sub
        self.taps_req = list(taps)
        nc = bass.Bass("TRN2", target_bir_lowering=False)
        self.nc = nc
        self.sc = Sched()
        d = nc.dram_tensor
        self.inp = {}
        for name, shape in [
            ("x", [T, C]),
            ("aqw", [C, C]), ("aqb", [C]), ("akw", [C, C]), ("akb", [C]),
            ("avw", [C, C]), ("avb", [C]), ("aow", [C, C]), ("aob", [C]),
            ("ipw", [3 * C, C]), ("ipb", [3 * C]), ("opw", [C, C]), ("opb", [C]),
            ("rw", [C, NE]), ("win", [NE, C, C]), ("bin", [NE, C]),
            ("wout", [NE, C, C]), ("bout", [NE, C]),
        ]:
            self.inp[name] = d(name, shape, F32, kind="ExternalInput").ap()
        self.inp["ident"] = d("ident", [128, 128], BF16, kind="ExternalInput").ap()
        self.inp["bmask"] = d("bmask", [128, 2048], BF16, kind="ExternalInput").ap()
        self.out = d("out", [T, C], F32, kind="ExternalOutput").ap()
        self.cur_set = None

    def T_(self, emit, **kw):
        return self.sc.step("tensor", emit, **kw)

    def S_(self, emit, **kw):
        return self.sc.step("scalar", emit, **kw)

    def V_(self, emit, **kw):
        return self.sc.step("vector", emit, **kw)

    def DMA(self, queue, emit, dsem="d0", **kw):
        return self.sc.step(queue, emit, dma=True, out_sem=dsem, **kw)

    def act_set(self, func):
        s = {AF.Exp: "exp", AF.Ln: "exp", AF.Gelu_apprx_tanh: "gelu"}.get(func)
        if s is not None and s != self.cur_set:
            self.cur_set = s
            from concourse.hw_specs import get_activation_tables
            names = list(get_activation_tables(self.nc.m.arch).keys())
            idx = names.index({"exp": "natural_log_exp_and_others",
                               "gelu": "gelu_apprx_tanh_and_others"}[s])

            def emit(e, idx=idx):
                e.add_instruction(mybir.InstLoadActFuncSet(
                    name=e.bass.get_next_instruction_name(),
                    act_func_set_id=idx, ins=[], outs=[]))
                return e.drain()
            self.S_(emit)

    def alloc(self, ctx):
        nc = self.nc
        E = ctx.enter_context
        B = {}
        B["x"] = E(nc.sbuf_tensor("xres", [128, NT * 1024], F32))
        B["xlt"] = E(nc.sbuf_tensor("xlt", [128, NCT * 2048], BF16))
        B["big"] = E(nc.sbuf_tensor("big", [128, NCT * 2048], BF16))
        B["pool"] = E(nc.sbuf_tensor("pool", [128, POOL_N], BF16))
        B["lnrow"] = E(nc.sbuf_tensor("lnrow", [128, 2 * 1024], BF16))
        B["ident"] = E(nc.sbuf_tensor("identsb", [128, 128], BF16))
        B["ones"] = E(nc.sbuf_tensor("ones", [1, 512], BF16))
        B["st"] = E(nc.sbuf_tensor("st", [128, 96], F32))
        B["kmT"] = E(nc.sbuf_tensor("kmT", [128, 16], F32))
        B["bcol"] = E(nc.sbuf_tensor("bcol", [128, 24], F32))
        B["bincol"] = E(nc.sbuf_tensor("bincol", [128, 64], BF16))
        B["bstage"] = E(nc.sbuf_tensor("bstage", [8, 1024], BF16))
        B["obias"] = E(nc.sbuf_tensor("obias", [1, 1024], BF16))
        B["probs"] = E(nc.sbuf_tensor("probs", [128, 128], F32))
        B["gates"] = E(nc.sbuf_tensor("gates", [128, 128], F32))
        B["r16"] = E(nc.sbuf_tensor("r16", [128, 32], F32))
        B["mx8"] = E(nc.sbuf_tensor("mx8", [128, 8], F32))
        B["rinv"] = E(nc.sbuf_tensor("rinv", [1, 512], BF16))
        B["rln"] = E(nc.sbuf_tensor("rln", [1, 512], BF16))
        B["consts"] = E(nc.sbuf_tensor("consts", [128, 2], F32))
        B["bcs"] = E(nc.sbuf_tensor("bcs", [64, 512], BF16))
        B["oplb"] = E(nc.sbuf_tensor("oplb", [128, 1024], BF16))
        B["psA"] = E(nc.psum_tensor("psA", [128, 1024], F32))
        B["psB"] = E(nc.psum_tensor("psB", [128, 1024], F32))
        B["psC"] = E(nc.psum_tensor("psC", [128, 512], F32))
        B["psD"] = E(nc.psum_tensor("psD", [128, 512], F32))
        B["psE"] = E(nc.psum_tensor("psE", [128, 512], F32))
        B["psT"] = E(nc.psum_tensor("psT", [128, 1024], BF16))
        self.B = B
        self.xv = B["x"][:, :].rearrange("p (i c) -> p i c", i=NT)
        self.xltv = B["xlt"][:, :].rearrange("p (k t) -> p k t", k=NCT)
        self.bigv = B["big"][:, :].rearrange("p (k t) -> p k t", k=NCT)

    def pool(self, lo, hi):
        return self.B["pool"][:, lo:hi]

    def prologue(self):
        B = self.B

        for g in range(4):
            def e_x(e, g=g):
                return e.dma_start(
                    out=self.xv[:, g * 4:(g + 1) * 4, :],
                    in_=self.inp["x"].rearrange("(i p) c -> p i c", p=128)
                    [:, g * 4:(g + 1) * 4, :])
            self.DMA("sync", e_x, dsem=f"dx{g}",
                     writes=[f"x:{i}" for i in range(g * 4, (g + 1) * 4)])

        self.DMA("sync", lambda e: e.dma_start(out=B["ident"][:, :], in_=self.inp["ident"]),
                 dsem="dident", writes=["ident"])
        self.V_(lambda e: e.memset(B["ones"][0:1, :], 1.0), writes=["ones"])
        self.V_(lambda e: e.memset(B["consts"][:, 0:1], EPS), writes=["consts"])
        self.V_(lambda e: e.memset(B["consts"][:, 1:2], -SHIFT), writes=["consts"])

    def ln_phase(self, kmask):
        B = self.B
        st = B["st"]

        def sl_(i):
            return (i % 6) * 16

        def s_bn(i):
            sl = sl_(i)
            xt = self.xv[:, i, :]

            def e_bn(e, xt=xt, sl=sl):
                e.bn_stats(st[:, sl + 4:sl + 10], xt[:, 0:512])
                return e.bn_stats(st[:, sl + 10:sl + 16], xt[:, 512:1024])
            self.V_(e_bn, reads=[f"x:{i}"], writes=[f"st:{i % 6}"])

        def s_ag(i):
            sl = sl_(i)

            def e_ag(e, sl=sl):
                return e.bn_aggr(st[:, sl + 0:sl + 2], st[:, sl + 4:sl + 16])
            self.V_(e_ag, reads=[f"st:{i % 6}"], writes=[f"st:{i % 6}"])

        def s_ln1(i):
            sl = sl_(i)

            def e_ln1(e, sl=sl):
                return e.activation(st[:, sl + 2:sl + 3], st[:, sl + 1:sl + 2],
                                    AF.Ln, bias=self.B["consts"][:, 0:1])
            self.S_(e_ln1, reads=[f"st:{i % 6}", "consts"], writes=[f"st:{i % 6}"])

        def s_ln2(i):
            sl = sl_(i)

            def e_ln2(e, sl=sl):
                return e.activation(st[:, sl + 3:sl + 4], st[:, sl + 2:sl + 3],
                                    AF.Exp, scale=-0.5)
            self.S_(e_ln2, reads=[f"st:{i % 6}"], writes=[f"st:{i % 6}"])

        def s_nm(i):
            sl = sl_(i)

            def e_nm(e, sl=sl):
                return e.tensor_scalar(st[:, sl + 0:sl + 1], st[:, sl + 0:sl + 1],
                                       st[:, sl + 3:sl + 4], -1.0,
                                       op0=ALU.mult, op1=ALU.mult)
            self.V_(e_nm, reads=[f"st:{i % 6}"], writes=[f"st:{i % 6}"])

        def s_ap(i):
            sl = sl_(i)
            xt = self.xv[:, i, :]
            lr = B["lnrow"][:, (i % 2) * 1024:(i % 2) * 1024 + 1024]

            def e_ap(e, xt=xt, lr=lr, sl=sl):
                return e.activation(lr, xt, AF.Identity, scale=st[:, sl + 3:sl + 4],
                                    bias=st[:, sl + 0:sl + 1])
            self.S_(e_ap, reads=[f"x:{i}", f"st:{i % 6}"], writes=[f"lnrow:{i % 2}"])
            if kmask:
                def e_km(e, lr=lr, i=i):
                    return e.tensor_scalar(B["kmT"][:, i:i + 1], lr[:, 0:1], 0.0,
                                           None, op0=ALU.not_equal)
                self.V_(e_km, reads=[f"lnrow:{i % 2}"], writes=["kmT"])

        def s_tr(i):
            lr = B["lnrow"][:, (i % 2) * 1024:(i % 2) * 1024 + 1024]

            def e_tr(e, lr=lr):
                ins = None
                for k in range(8):
                    ins = e.transpose(B["psT"][:, k * 128:(k + 1) * 128],
                                      lr[:, k * 128:(k + 1) * 128], B["ident"][:, :])
                return ins
            self.T_(e_tr, reads=[f"lnrow:{i % 2}", "ident"], writes=["psT"])

            def e_ev(e, i=i):
                src = B["psT"][:, :].rearrange("p (k t) -> p k t", k=8)
                dst = self.xltv[:, :, i * 128:(i + 1) * 128]
                return e.tensor_copy(dst, src)
            self.V_(e_ev, reads=["psT"], writes=[f"xlt:{i // 4}", "psT"])

        for it in range(NT + 5):
            if it < NT:
                s_bn(it)
            if 1 <= it < NT + 1:
                s_ag(it - 1)
            if 2 <= it < NT + 2:
                s_ln1(it - 2)
            if 3 <= it < NT + 3:
                s_ln2(it - 3)
            if 4 <= it < NT + 4:
                s_nm(it - 4)
            if it >= 5:
                s_ap(it - 5)
                s_tr(it - 5)

    def attn_phase(self, cfg):
        B = self.B
        nc = self.nc
        band = cfg["band"]
        ob_ap = self.inp[cfg["ob"]].rearrange("(o c) -> o c", o=1)
        self.DMA("gpsimd", lambda e, a=ob_ap: e.dma_start(out=B["obias"][0:1, :], in_=a),
                 dsem="dob", writes=["obias"])
        vv = self.pool(P_VAUG, P_VAUG + 16 * 130).rearrange("p (t s) -> p t s", t=16)
        wx = ["wi:0a"] if cfg.get("fence_moe") else []
        self.V_(lambda e, v=vv: e.memset(v[:, :, 64:65], 1.0), writes=["vaug"] + wx)
        self.V_(lambda e, v=vv: e.memset(v[:, :, 129:130], 1.0), writes=["vaug"])
        q0 = self.pool(P_QT, P_QT + 2048)
        q1 = self.pool(P_Q1, P_Q1 + 2048)
        self.V_(lambda e, a=q0: e.memset(a[64:128, :], 0.0), writes=["qT"])
        self.V_(lambda e, a=q1: e.memset(a[0:64, :], 0.0), writes=["qT"])
        if cfg["band"]:
            mask_ap = self.pool(P_MASK, P_MASK + 2048)
            self.DMA("sync", lambda e, m=mask_ap: e.dma_start(out=m, in_=self.inp["bmask"]),
                     dsem="dmask", writes=["mask"])
        self.bias_cols_all(cfg)
        for p in range(9):
            if p < 8:
                self.wn_load(cfg, p)
            if p == 1:
                self.wt_transpose(0)
            if p > 0:
                self.qkv_pair(cfg, p - 1)
                self.core_pair(cfg, p - 1, band,
                               host_wt=(p if p < 8 else None))
        self.o_proj(cfg)

    def wn_load(self, cfg, p):
        b = p % 2
        fence = list(MOE_POOL_KEYS) if (p == 0 and cfg.get("fence_moe")) else []
        dst = self.pool(P_WN + b * 3072, P_WN + b * 3072 + 3072)
        dstv = dst.rearrange("p (w c) -> p w c", w=3)
        for w, src in enumerate(cfg["qkv_w"]):
            ap = src[p * 128:(p + 1) * 128, :]

            def e(e_, d=dstv[:, w, :], a=ap):
                return e_.dma_start(out=d, in_=a)
            self.DMA("gpsimd", e, dsem=f"dwn{b}", writes=[f"wn:{b}"] + fence)
            fence = []

    def bias_cols_all(self, cfg):
        B = self.B
        for w, srcb in enumerate(cfg["qkv_b"]):
            ap = srcb.rearrange("(k p) -> k p", k=8)

            def e(e_, d=B["bstage"][0:8, w * 128:(w + 1) * 128], a=ap):
                return e_.dma_start(out=d, in_=a)
            self.DMA("gpsimd", e, dsem="dbst0", writes=["bst"])

        def e_tr(e):
            ins = None
            for w in range(3):
                ins = e.transpose(B["psT"][:, w * 8:(w + 1) * 8],
                                  B["bstage"][0:8, w * 128:(w + 1) * 128],
                                  B["ident"][0:8, 0:8])
            return ins
        self.T_(e_tr, reads=["bst", "ident"], writes=["psT"])

        def e_ev(e):
            return e.tensor_copy(B["bcol"][:, 0:24], B["psT"][:, 0:24])
        self.V_(e_ev, reads=["psT"], writes=["bcol", "psT"])

    def wt_transpose(self, p, only_w=None):
        B = self.B
        b = p % 2
        src = self.pool(P_WN + b * 3072, P_WN + b * 3072 + 3072)
        srcv = src.rearrange("p (w c) -> p w c", w=3)
        dst = self.pool(P_WT + b * 3072, P_WT + b * 3072 + 3072)
        dstv = dst.rearrange("p (w k d) -> p w k d", w=3, k=8)
        for w in (range(3) if only_w is None else [only_w]):
            def e_tr(e, w=w, b=b):
                ins = None
                for k in range(8):
                    ins = e.transpose(B["psT"][:, k * 128:(k + 1) * 128],
                                      srcv[:, w, k * 128:(k + 1) * 128],
                                      B["ident"][:, :])
                return ins
            self.T_(e_tr, reads=[f"wn:{b}", "ident"], writes=["psT"])

            def e_ev(e, w=w):
                return e.tensor_copy(dstv[:, w, :, :],
                                     B["psT"][:, :].rearrange("p (k d) -> p k d", k=8))
            self.V_(e_ev, reads=["psT"], writes=[f"wt:{b}", "psT"])

    def qkv_pair(self, cfg, p):
        B = self.B
        b = p % 2
        wtv = self.pool(P_WT + b * 3072, P_WT + b * 3072 + 3072).rearrange(
            "p (w k d) -> p w k d", w=3, k=8)
        q0 = self.pool(P_QT, P_QT + 2048)
        q1 = self.pool(P_Q1, P_Q1 + 2048)
        kT = self.pool(P_KT, P_KT + 2048)
        ureg = self.pool(P_U, P_U + 8192)
        dests = [("qT", None), ("kT", kT), ("vT", ureg[:, 0:2048])]
        for w, (dkey, dap) in enumerate(dests):
            for half in range(2):
                ps = B["psA"] if half == 0 else B["psB"]
                pskeys = ["psA0", "psA1"] if half == 0 else ["psB0", "psB1"]
                for cc in range(2):
                    chunk = half * 2 + cc

                    def e_mm(e, w=w, cc=cc, chunk=chunk, ps=ps, b=b):
                        ins = None
                        for k in range(8):
                            ins = e.matmul(ps[:, cc * 512:(cc + 1) * 512],
                                           wtv[:, w, k, :],
                                           self.xltv[:, k, chunk * 512:(chunk + 1) * 512],
                                           start=(k == 0), stop=(k == 7))
                        return ins
                    self.T_(e_mm, reads=[f"wt:{b}", f"xlt:{chunk}"],
                            writes=[pskeys[cc]])
                wkeys = [dkey] if dkey != "vT" else ["U:0", "U:1", "U:2", "U:3"]

                bc = w * 8 + p

                def e_ev(e, dap=dap, half=half, ps=ps, bc=bc):
                    if dap is None:
                        e.tensor_scalar(q0[0:64, half * 1024:(half + 1) * 1024],
                                        ps[0:64, :], B["bcol"][0:64, bc:bc + 1],
                                        None, op0=ALU.add)
                        return e.tensor_scalar(
                            q1[64:128, half * 1024:(half + 1) * 1024],
                            ps[64:128, :], B["bcol"][64:128, bc:bc + 1],
                            None, op0=ALU.add)
                    return e.tensor_scalar(dap[:, half * 1024:(half + 1) * 1024],
                                           ps[:, :], B["bcol"][:, bc:bc + 1],
                                           None, op0=ALU.add)
                self.V_(e_ev, reads=pskeys + ["bcol"], writes=wkeys + pskeys)
        vv = self.pool(P_VAUG, P_VAUG + 16 * 130).rearrange("p (t s) -> p t s", t=16)
        for half in range(2):
            def e_tr(e, half=half):
                ins = None
                for k in range(8):
                    ins = e.transpose(B["psT"][:, k * 128:(k + 1) * 128],
                                      ureg[:, half * 1024 + k * 128:half * 1024 + (k + 1) * 128],
                                      B["ident"][:, :])
                return ins
            self.T_(e_tr, reads=["U:0", "U:1", "U:2", "U:3", "ident"], writes=["psT"])

            def e_ev(e, half=half):
                src = B["psT"][:, :].rearrange("p (t g x) -> p t g x", t=8, g=2)
                dst = vv[:, half * 8:(half + 1) * 8, :].rearrange(
                    "p t (g y) -> p t g y", g=2)[:, :, :, 0:64]
                return e.tensor_copy(dst, src)
            self.V_(e_ev, reads=["psT"], writes=["vaug", "psT"])

    def core_pair(self, cfg, p, band, host_wt=None):
        B = self.B
        qbufs = (self.pool(P_QT, P_QT + 2048), self.pool(P_Q1, P_Q1 + 2048))
        kT = self.pool(P_KT, P_KT + 2048)
        ureg = self.pool(P_U, P_U + 8192)
        uv = ureg.rearrange("p (h s) -> p h s", h=2)
        vv = self.pool(P_VAUG, P_VAUG + 16 * 130).rearrange("p (t s) -> p t s", t=16)
        maskv = self.pool(P_MASK, P_MASK + 2048)
        self.act_set(AF.Exp)
        R = 8
        for j in range(4):
            if host_wt is not None and j >= 1:
                self.wt_transpose(host_wt, only_w=j - 1)
            tiles = btiles(j) if band else [(i, None) for i in range(16)]
            n = len(tiles)

            def ukeys(g):
                return [f"U:{g % R}", f"U:{8 + g % R}"]

            def emit_scores(g):
                i, mi = tiles[g]
                ps = B["psA"] if g % 2 == 0 else B["psB"]
                pk = "psA" if g % 2 == 0 else "psB"
                sl = (g % R) * 512

                def e_mm(e, i=i, j=j, ps=ps):
                    e.matmul(ps[:, 0:512], kT[:, i * 128:(i + 1) * 128],
                             qbufs[0][:, j * 512:(j + 1) * 512],
                             start=True, stop=True)
                    return e.matmul(ps[:, 512:1024], kT[:, i * 128:(i + 1) * 128],
                                    qbufs[1][:, j * 512:(j + 1) * 512],
                                    start=True, stop=True)
                self.T_(e_mm, reads=["qT", "kT"], writes=[pk + "0", pk + "1"])
                uout = uv[:, :, sl:sl + 512]

                def e_ex(e, ps=ps, uout=uout):
                    return e.activation(uout, ps[:, :], AF.Exp, scale=0.125,
                                        bias=self.B["consts"][:, 1:2])
                self.S_(e_ex, reads=[pk + "0", pk + "1", "consts"],
                        writes=ukeys(g) + [pk + "0", pk + "1"])
                if not band:
                    def e_mk(e, uout=uout, i=i):
                        return e.tensor_scalar(uout, uout, B["kmT"][:, i:i + 1],
                                               None, op0=ALU.mult)
                    self.V_(e_mk, reads=ukeys(g) + ["kmT"], writes=ukeys(g))
                elif mi is not None:
                    def e_mkb(e, uout=uout, mi=mi):
                        e.tensor_tensor(uout[:, 0, :], uout[:, 0, :],
                                        maskv[:, mi * 512:(mi + 1) * 512],
                                        op=ALU.mult)
                        return e.tensor_tensor(uout[:, 1, :], uout[:, 1, :],
                                               maskv[:, mi * 512:(mi + 1) * 512],
                                               op=ALU.mult)
                    self.V_(e_mkb, reads=ukeys(g) + ["mask"], writes=ukeys(g))

            def emit_pv(g):
                i, mi = tiles[g]
                sl = (g % R) * 512

                def e_pv(e, g=g, i=i, sl=sl, n=n):
                    e.matmul(B["psC"][0:65, :], vv[:, i, 0:65],
                             uv[:, 0, sl:sl + 512],
                             start=(g == 0), stop=(g == n - 1))
                    return e.matmul(B["psD"][0:65, :], vv[:, i, 65:130],
                                    uv[:, 1, sl:sl + 512],
                                    start=(g == 0), stop=(g == n - 1))
                self.T_(e_pv, reads=["vaug"] + ukeys(g), writes=["psC", "psD"])

            for g in range(n):
                emit_scores(g)
                if g >= 2:
                    emit_pv(g - 2)
            emit_pv(n - 2)
            emit_pv(n - 1)
            for h, pvk in ((0, "psC"), (1, "psD")):
                pv = B[pvk]

                def e_ri1(e, pv=pv):
                    return e.activation(B["rln"][0:1, :], pv[64:65, :], AF.Ln)
                self.S_(e_ri1, reads=[pvk], writes=["rln"])

                def e_ri2(e):
                    with self.nc.allow_low_precision(
                            reason="bf16 softmax denominator for bf16 PV product"):
                        return e.activation(B["rinv"][0:1, :], B["rln"][0:1, :],
                                            AF.Exp, scale=-1.0)
                self.S_(e_ri2, reads=["rln"], writes=["rinv"])

                def e_bc(e):
                    return e.matmul(B["psE"][0:64, :], B["ones"][0:1, 0:64],
                                    B["rinv"][0:1, :], start=True, stop=True)
                self.T_(e_bc, reads=["rinv", "ones"], writes=["psE"])

                def e_cp(e):
                    return e.tensor_copy(B["bcs"][0:64, :], B["psE"][0:64, :])
                self.V_(e_cp, reads=["psE"], writes=["bcs", "psE"])

                def e_mu(e, pv=pv, p=p, h=h, j=j):
                    dst = self.bigv[h * 64:(h + 1) * 64, p, j * 512:(j + 1) * 512]
                    return e.tensor_tensor(dst, pv[0:64, :], B["bcs"][0:64, :],
                                           op=ALU.mult)
                self.V_(e_mu, reads=[pvk, "bcs"],
                        writes=[f"big:{4 * j + q}" for q in range(4)] + [pvk])

    def o_proj(self, cfg):
        B = self.B
        wof = cfg["ow"]
        woT = self.pool(P_U, P_U + 8192).rearrange("p (k d) -> p k d", k=8)
        ukeys = [f"U:{i}" for i in range(16)]
        for s in range(8):
            b = s % 2
            dst = self.pool(P_WN + b * 3072, P_WN + b * 3072 + 1024)

            def e_dma(e, dst=dst, s=s):
                return e.dma_start(out=dst, in_=wof[s * 128:(s + 1) * 128, :])
            self.DMA("gpsimd", e_dma, dsem=f"dwn{b}", writes=[f"wn:{b}"])

            def e_tr(e, dst=dst):
                ins = None
                for k in range(8):
                    ins = e.transpose(B["psT"][:, k * 128:(k + 1) * 128],
                                      dst[:, k * 128:(k + 1) * 128], B["ident"][:, :])
                return ins
            self.T_(e_tr, reads=[f"wn:{b}", "ident"], writes=["psT"])

            def e_ev(e, s=s):
                src = B["psT"][:, :].rearrange("p (k d) -> p k d", k=8)
                return e.tensor_copy(woT[:, :, s * 128:(s + 1) * 128], src)
            self.V_(e_ev, reads=["psT"],
                    writes=["psT", "woT"] + (ukeys if s == 0 else []))
        opl = B["oplb"][:, :]
        for db in range(2):
            def e_bc(e, db=db):
                return e.matmul(B["psE"][:, :], B["ones"][0:1, 0:128],
                                B["obias"][0:1, db * 512:(db + 1) * 512],
                                start=True, stop=True)
            self.T_(e_bc, reads=["obias", "ones"], writes=["psE"])

            def e_cp(e, db=db):
                return e.tensor_copy(opl[:, db * 512:(db + 1) * 512], B["psE"][:, :])
            self.V_(e_cp, reads=["psE"], writes=["oplb", "psE"])
        for tt in range(NT):
            ps = B["psA"] if tt % 2 == 0 else B["psB"]
            pk = "psA" if tt % 2 == 0 else "psB"
            for db in range(2):
                def e_mm(e, tt=tt, db=db, ps=ps):
                    ins = None
                    for k in range(8):
                        ins = e.matmul(ps[:, db * 512:(db + 1) * 512],
                                       self.bigv[:, k, tt * 128:(tt + 1) * 128],
                                       woT[:, k, db * 512:(db + 1) * 512],
                                       start=(k == 0), stop=(k == 7))
                    return ins
                self.T_(e_mm, reads=["woT", f"big:{tt}"]
                        + [f"U:{i}" for i in range(16)],
                        writes=[pk + str(db)])

            def e_add1(e, tt=tt, ps=ps):
                return e.tensor_tensor(self.xv[:, tt, :], self.xv[:, tt, :],
                                       ps[:, :], op=ALU.add)
            self.V_(e_add1, reads=[pk + "0", pk + "1", f"x:{tt}"],
                    writes=[f"x:{tt}", pk + "0", pk + "1"])

            def e_add2(e, tt=tt):
                return e.tensor_tensor(self.xv[:, tt, :], self.xv[:, tt, :],
                                       opl[:, :], op=ALU.add)
            self.V_(e_add2, reads=[f"x:{tt}", "oplb"], writes=[f"x:{tt}"])

    def moe_phase(self, first):
        B = self.B
        wi0a = self.pool(0, 6144).rearrange("p (k h) -> p k h", k=6)
        wi0b = self.pool(P_Q1, P_Q1 + 2048).rearrange("p (k h) -> p k h", k=2)
        wi1 = self.pool(8192, 16384).rearrange("p (k h) -> p k h", k=8)

        def wi_tile(b, k, mt):
            if b == 1:
                return wi1[:, k, mt * 128:(mt + 1) * 128]
            if k < 6:
                return wi0a[:, k, mt * 128:(mt + 1) * 128]
            return wi0b[:, k - 6, mt * 128:(mt + 1) * 128]
        wov = self.pool(16384, 16384 + 8192).rearrange("p (k d) -> p k d", k=8)
        rwv = self.pool(24576, 24576 + 64).rearrange("p (k e) -> p k e", k=8)
        fence = list(ATTN_POOL_KEYS)

        def dma_wi(e_idx):
            b = e_idx % 2
            nonlocal fence
            if b == 1:
                def e(e_, e_idx=e_idx):
                    return e_.dma_start(out=wi1[:, :, :],
                                        in_=self.inp["win"][e_idx:e_idx + 1, :, :]
                                        .rearrange("o (k p) h -> p (o k) h", p=128))
                self.DMA("gpsimd", e, dsem="dwi1", writes=["wi:1"] + fence)
            else:
                def ea(e_, e_idx=e_idx):
                    return e_.dma_start(out=wi0a[:, :, :],
                                        in_=self.inp["win"][e_idx:e_idx + 1, 0:768, :]
                                        .rearrange("o (k p) h -> p (o k) h", p=128))
                self.DMA("gpsimd", ea, dsem="dwi0",
                         writes=["wi:0a", "qT", "kT", "vaug"] + fence)

                def eb(e_, e_idx=e_idx):
                    return e_.dma_start(out=wi0b[:, :, :],
                                        in_=self.inp["win"][e_idx:e_idx + 1, 768:1024, :]
                                        .rearrange("o (k p) h -> p (o k) h", p=128))
                self.DMA("gpsimd", eb, dsem="dwi0b", writes=["wi:0b"])
            fence = []

        def bias_all():
            def e_bi(e_):
                return e_.dma_start(
                    out=B["bstage"][0:8, :].rearrange("k (e p) -> k e p", e=8),
                    in_=self.inp["bin"].rearrange("e (k p) -> k e p", k=8))
            self.DMA("gpsimd", e_bi, dsem="dbst1", writes=["bst"])

            def e_tr(e_):
                ins = None
                for ei in range(8):
                    ins = e_.transpose(B["psT"][:, ei * 8:(ei + 1) * 8],
                                       B["bstage"][0:8, ei * 128:(ei + 1) * 128],
                                       B["ident"][0:8, 0:8])
                return ins
            self.T_(e_tr, reads=["bst", "ident"], writes=["psT"])

            def e_ev(e_):
                return e_.tensor_copy(B["bincol"][:, 0:64], B["psT"][:, 0:64])
            self.V_(e_ev, reads=["psT"], writes=["binc", "psT"])


        def e_rw(e_):
            return e_.dma_start(out=rwv[:, :, :],
                                in_=self.inp["rw"].rearrange("(k p) e -> p k e", p=128))
        self.DMA("gpsimd", e_rw, dsem="drw", writes=["rw"] + fence)
        fence = []
        dma_wi(0)
        bias_all()
        for tt in range(NT):
            def e_mm(e, tt=tt):
                ins = None
                for k in range(8):
                    ins = e.matmul(B["psC"][:, tt * 8:(tt + 1) * 8],
                                   self.xltv[:, k, tt * 128:(tt + 1) * 128],
                                   rwv[:, k, :], start=(k == 0), stop=(k == 7))
                return ins
            self.T_(e_mm, reads=["rw", f"xlt:{tt // 4}"], writes=["psC"])
        self.act_set(AF.Exp)
        self.S_(lambda e: e.activation(B["probs"][:, :], B["psC"][:, 0:128], AF.Exp),
                reads=["psC"], writes=["probs", "psC"])
        pv = B["probs"][:, :].rearrange("p (t e) -> p t e", t=16)
        self.V_(lambda e: e.reduce_sum(B["r16"][:, 0:16], pv, AX.X),
                reads=["probs"], writes=["r16"])
        self.V_(lambda e: e.reciprocal(B["r16"][:, 16:32], B["r16"][:, 0:16]),
                reads=["r16"], writes=["r16"])
        gT = B["pool"][0:8, P_MASK:P_MASK + 2048]
        boall = B["pool"][0:8, 24640:25664]
        gbf = B["lnrow"][:, 1024:1152]

        def e_ba(e_):
            return e_.dma_start(out=boall, in_=self.inp["bout"])
        self.DMA("gpsimd", e_ba, dsem="dbo", writes=["wt:1"])
        for tt in range(NT):
            def e_g1(e, tt=tt):
                return e.max(out=B["mx8"][:, :], in_=B["probs"][:, tt * 8:(tt + 1) * 8])
            self.V_(e_g1, reads=["probs"], writes=["mx8"])

            def e_g2(e, tt=tt):
                return e.scalar_tensor_tensor(
                    B["gates"][:, tt * 8:(tt + 1) * 8],
                    B["probs"][:, tt * 8:(tt + 1) * 8], B["mx8"][:, 1:2],
                    B["probs"][:, tt * 8:(tt + 1) * 8], op0=ALU.is_ge, op1=ALU.mult)
            self.V_(e_g2, reads=["probs", "mx8"], writes=["gates"])

            def e_g3(e, tt=tt):
                return e.tensor_scalar(B["gates"][:, tt * 8:(tt + 1) * 8],
                                       B["gates"][:, tt * 8:(tt + 1) * 8],
                                       B["r16"][:, 16 + tt:17 + tt], None, op0=ALU.mult)
            self.V_(e_g3, reads=["gates", "r16"], writes=["gates"])
        def build_gT():
            self.V_(lambda e: e.tensor_copy(gbf, B["gates"][:, :]),
                    reads=["gates"], writes=["lnrow:1"])
            for fill in range(2):
                def e_tr(e, fill=fill):
                    ins = None
                    for i in range(8):
                        tt = fill * 8 + i
                        ins = e.transpose(B["psT"][0:8, i * 128:(i + 1) * 128],
                                          gbf[:, tt * 8:(tt + 1) * 8],
                                          B["ident"][:, :])
                    return ins
                self.T_(e_tr, reads=["lnrow:1", "ident"], writes=["psT"])

                def e_ev(e, fill=fill):
                    return e.tensor_copy(gT[0:8, fill * 1024:(fill + 1) * 1024],
                                         B["psT"][0:8, 0:1024])
                self.V_(e_ev, reads=["psT"], writes=["mask", "psT"])

        for ei in range(NE):
            b = ei % 2
            def e_wo(e_, ei=ei):
                return e_.dma_start(out=wov[:, :, :],
                                    in_=self.inp["wout"][ei:ei + 1, :, :].rearrange(
                                        "o (k p) d -> p (o k) d", p=128))
            self.DMA("gpsimd", e_wo, dsem="dwo", writes=["wo_e"])
            if ei + 1 < NE:
                dma_wi(ei + 1)
            self.act_set(AF.Gelu_apprx_tanh)
            for mt in range(8):
                for half in range(2):
                    ps = B["psA"] if half == 0 else B["psB"]
                    pk = "psA" if half == 0 else "psB"
                    for cc in range(2):
                        chunk = half * 2 + cc

                        def e_mm(e, mt=mt, cc=cc, chunk=chunk, ps=ps, b=b):
                            ins = None
                            for k in range(8):
                                ins = e.matmul(ps[:, cc * 512:(cc + 1) * 512],
                                               wi_tile(b, k, mt),
                                               self.xltv[:, k, chunk * 512:(chunk + 1) * 512],
                                               start=(k == 0), stop=(k == 7))
                            return ins
                        self.T_(e_mm,
                                reads=(["wi:1"] if b == 1 else ["wi:0a", "wi:0b"])
                                + [f"xlt:{chunk}"],
                                writes=[pk + str(cc)])

                    def e_gl(e, mt=mt, half=half, ps=ps, ei=ei):
                        return e.activation(self.bigv[:, mt, half * 1024:(half + 1) * 1024],
                                            ps[:, :], AF.Gelu_apprx_tanh,
                                            bias=B["bincol"][:, ei * 8 + mt:ei * 8 + mt + 1])
                    self.S_(e_gl, reads=[pk + "0", pk + "1", "binc"],
                            writes=[pk + "0", pk + "1"]
                            + [f"big:{half * 8 + q}" for q in range(8)])
            ykeys = ("psC", "psD", "psE")
            for tt in range(NT):
                for db in range(2):
                    yk = ykeys[(tt * 2 + db) % 3]
                    ps = B[yk]

                    def e_mm(e, tt=tt, db=db, ps=ps):
                        ins = None
                        for k in range(8):
                            ins = e.matmul(ps[:, :],
                                           self.bigv[:, k, tt * 128:(tt + 1) * 128],
                                           wov[:, k, db * 512:(db + 1) * 512],
                                           start=(k == 0), stop=(k == 7))
                        return ins
                    self.T_(e_mm, reads=["wo_e", f"big:{tt}"], writes=[yk])

                    def e_acc(e, tt=tt, db=db, ps=ps, ei=ei):
                        dst = self.xv[:, tt, db * 512:(db + 1) * 512]
                        return e.scalar_tensor_tensor(
                            dst, ps[:, :], B["gates"][:, tt * 8 + ei:tt * 8 + ei + 1],
                            dst, op0=ALU.mult, op1=ALU.add)
                    self.V_(e_acc, reads=[yk, "gates", f"x:{tt}"],
                            writes=[f"x:{tt}", yk])
        for tt in range(NT):
            for db in range(2):
                yk = ykeys[(tt * 2 + db) % 3]
                ps = B[yk]

                def e_mm(e, tt=tt, db=db, ps=ps, gT=gT, boall=boall):
                    return e.matmul(ps[:, :], gT[0:8, tt * 128:(tt + 1) * 128],
                                    boall[0:8, db * 512:(db + 1) * 512],
                                    start=True, stop=True)
                self.T_(e_mm, reads=["mask", "wt:1"], writes=[yk])

                def e_add(e, tt=tt, db=db, ps=ps):
                    dst = self.xv[:, tt, db * 512:(db + 1) * 512]
                    return e.tensor_tensor(dst, dst, ps[:, :], op=ALU.add)
                self.V_(e_add, reads=[yk, f"x:{tt}"], writes=[f"x:{tt}", yk])

    def tap_spec(self, name):
        B = self.B
        specs = {
            "st": (B["st"][:, :], [128, 64], F32),
            "kmT": (B["kmT"][:, :], [128, 16], F32),
            "consts": (B["consts"][:, :], [128, 2], F32),
            "xlt0": (B["xlt"][:, 0:2048], [128, 2048], BF16),
            "xlt": (B["xlt"][:, :], [128, NCT * 2048], BF16),
            "lnrow": (B["lnrow"][:, :], [128, 2048], BF16),
            "qT": (self.pool(P_QT, P_QT + 2048), [128, 2048], BF16),
            "kT": (self.pool(P_KT, P_KT + 2048), [128, 2048], BF16),
            "vaug": (self.pool(P_VAUG, P_VAUG + 2080), [128, 2080], BF16),
            "u": (self.pool(P_U, P_U + 8192), [128, 8192], BF16),
            "wt0": (self.pool(P_WT, P_WT + 3072), [128, 3072], BF16),
            "bcol": (B["bcol"][:, :], [128, 8], BF16),
            "big": (B["big"][:, :], [128, NCT * 2048], BF16),
            "probs": (B["probs"][:, :], [128, 128], F32),
            "gates": (B["gates"][:, :], [128, 128], F32),
            "brow": (B["brow"][:, :], [1, 2048], BF16),
            "bincol": (B["bincol"][:, :], [128, 16], BF16),
        }
        return specs[name]

    def epilogue(self):
        def e_out(e):
            return e.dma_start(out=self.out.rearrange("(i p) c -> p i c", p=128),
                               in_=self.xv)
        self.DMA("sync", e_out, dsem="Ou", reads=[f"x:{i}" for i in range(NT)])
        allkeys = list(self.sc.lw.keys())
        for name in self.taps_req:
            ap, shape, dt = self.tap_spec(name)
            o = self.nc.dram_tensor("tap_" + name, shape, dt,
                                    kind="ExternalOutput").ap()

            def e_t(e, o=o, a=ap):
                return e.dma_start(out=o, in_=a)
            self.DMA("sync", e_t, dsem="Ou", reads=allkeys)

    def build(self):
        nc = self.nc
        with ExitStack() as ctx:
            self.alloc(ctx)
            self.prologue()
            cfg_a = dict(band=False, fence_moe=False,
                         qkv_w=[self.inp["aqw"], self.inp["akw"], self.inp["avw"]],
                         qkv_b=[self.inp["aqb"], self.inp["akb"], self.inp["avb"]],
                         ow=self.inp["aow"], ob="aob")
            ip, ipb = self.inp["ipw"], self.inp["ipb"]
            cfg_b = dict(band=True, fence_moe=True,
                         qkv_w=[ip[0:C, :], ip[C:2 * C, :], ip[2 * C:3 * C, :]],
                         qkv_b=[ipb[0:C], ipb[C:2 * C], ipb[2 * C:3 * C]],
                         ow=self.inp["opw"], ob="opb")
            subs = [("attn", cfg_a), ("moe", None), ("attn", cfg_b), ("moe", None)]
            for si, (kind, cfg) in enumerate(subs[: self.n_sub]):
                self.act_set(AF.Ln)
                self.ln_phase(kmask=(kind == "attn" and not (cfg or {}).get("band")))
                if kind == "attn":
                    self.attn_phase(cfg)
                else:
                    self.moe_phase(first=(si == 1))
            self.epilogue()

            S = {}
            for s in self.sc.cnt:
                S[s] = ctx.enter_context(nc.semaphore(s))
            blk = ctx.enter_context(nc.Block())

            def make(eng_name):
                steps = self.sc.prog[eng_name]

                own = SEM_OF[eng_name]

                def body(e):
                    for waits, emit, sem, val, need_drain, self_wait in steps:
                        for s, v in waits:
                            e.wait_ge(S[s], v)
                        if self_wait:
                            e.wait_ge(S[own], self_wait)
                        if need_drain:
                            e.drain()
                        ins = emit(e)
                        if ins is None:
                            ins = e.nop()
                        ins.then_inc(S[sem], 1 if sem in ("Tm", "Sc", "Ve") else 16)
                    if eng_name == "sync":
                        e.wait_ge(S["Ou"], self.sc.cnt["Ou"])
                return body
            blk.tensor(make("tensor"))
            blk.scalar(make("scalar"))
            blk.vector(make("vector"))
            blk.sync(make("sync"))
            blk.gpsimd(make("gpsimd"))
        return nc


def _masks():
    m = np.zeros((128, 2048), dtype=np.float32)
    for mi, d in enumerate((0, -128, -256, -384)):
        for i in range(128):
            m[i, mi * 512:(mi + 1) * 512] = (i <= np.arange(512) + d)
    return m.astype(ml_dtypes.bfloat16)


_CACHE = {}


def _get_nc(n_sub=4, taps=()):
    key = (n_sub, tuple(taps))
    if key not in _CACHE:
        _CACHE[key] = MK(n_sub, taps).build()
    return _CACHE[key]


def kernel(**inputs):
    nc = _get_nc(4)
    base = {
        "aqw": inputs["L0_attn_a_wq_w"], "aqb": inputs["L0_attn_a_wq_b"],
        "akw": inputs["L0_attn_a_wk_w"], "akb": inputs["L0_attn_a_wk_b"],
        "avw": inputs["L0_attn_a_wv_w"], "avb": inputs["L0_attn_a_wv_b"],
        "aow": inputs["L0_attn_a_wo_w"], "aob": inputs["L0_attn_a_wo_b"],
        "ipw": inputs["L0_attn_b_in_proj_w"], "ipb": inputs["L0_attn_b_in_proj_b"],
        "opw": inputs["L0_attn_b_out_proj_w"], "opb": inputs["L0_attn_b_out_proj_b"],
        "rw": inputs["L0_ffn_router_w"], "win": inputs["L0_ffn_w_in"],
        "bin": inputs["L0_ffn_b_in"], "wout": inputs["L0_ffn_w_out"],
        "bout": inputs["L0_ffn_b_out"],
        "ident": np.eye(128, dtype=np.float32).astype(ml_dtypes.bfloat16),
        "bmask": _masks(),
    }
    base = {k: np.ascontiguousarray(v) for k, v in base.items()}
    x = inputs["L0_attn_a_x"]
    in_maps = []
    for i in range(8):
        m = dict(base)
        m["x"] = np.ascontiguousarray(x[i])
        in_maps.append(m)
    res = run_bass_kernel_spmd(nc, in_maps, core_ids=list(range(8)))
    return np.stack([np.asarray(r["out"]) for r in res.results]).astype(np.float32)
